# version 14
# baseline (speedup 1.0000x reference)
import numpy as np
import ml_dtypes
import concourse.bass as bass
import concourse.mybir as mybir
from concourse.bass_utils import run_bass_kernel_spmd
from contextlib import ExitStack

F32 = mybir.dt.float32
BF16 = mybir.dt.bfloat16
U8 = mybir.dt.uint8
I32 = mybir.dt.int32
OP = mybir.AluOpType
AF = mybir.ActivationFunctionType
AX = mybir.AxisListType.X

D = 2048
ND = 16
CH = 128
EPS = 1e-5
MAGIC = 1597463007.0
N_CORES = 8
DEBUG_TAPS = False
TRUNC = 0
NEWTON_ITERS = 2


class Prog:
    ENGS = ("sync", "tensor", "vector", "scalar", "gpsimd")
    BARRIER_DMA = ("d_w", "d_w2")

    def __init__(self):
        self.ops = []

    def op(self, eng, deps, emit, inc=True, dma_sem=None):
        oid = len(self.ops)
        self.ops.append(dict(eng=eng, deps=[d for d in deps if d is not None],
                             emit=emit, inc=inc, dma=dma_sem))
        return oid

    def finalize(self):
        cnt = {e: 0 for e in self.ENGS}
        dma_cnt = {}
        for o in self.ops:
            if o["dma"] is not None:
                dma_cnt[o["dma"]] = dma_cnt.get(o["dma"], 0) + 1
                o["val"] = dma_cnt[o["dma"]] * 16
            elif o["inc"]:
                cnt[o["eng"]] += 1
                o["val"] = cnt[o["eng"]]
            else:
                o["val"] = None
        for i, o in enumerate(self.ops):
            waits = {}
            for d in o["deps"]:
                p = self.ops[d]
                assert d < i, f"dep {d} !< {i}"
                if (p["eng"] == o["eng"] == "tensor" and p["dma"] is None):
                    continue
                key = p["dma"] if p["dma"] is not None else p["eng"]
                assert p["val"] is not None, f"op {d} awaited but inc=False"
                waits[key] = max(waits.get(key, 0), p["val"])
            o["waits"] = waits
        for o in self.ops:
            for k in list(o["waits"]):
                if k in self.BARRIER_DMA:
                    o["waits"][k] = dma_cnt[k] * 16

    def emit_engine(self, ename, eng, sems):
        for o in self.ops:
            if o["eng"] != ename:
                continue
            for k, v in o["waits"].items():
                eng.wait_ge(sems[k], v)
            inst = o["emit"](eng)
            if o["dma"] is not None:
                inst.then_inc(sems[o["dma"]], 16)
            elif o["inc"]:
                inst.then_inc(sems[ename], 1)


def build_nc(n_seq, seq_len):
    TOK = n_seq * seq_len
    NCH = TOK // CH
    CPS = seq_len // CH
    NQ = 4

    nc = bass.Bass("TRN2", target_bir_lowering=False)

    dx = nc.dram_tensor("L0_attn_x", [n_seq, seq_len, D], F32, kind="ExternalInput").ap()
    dgam = nc.dram_tensor("L0_attn_ln_gamma", [D], F32, kind="ExternalInput").ap()
    dbet = nc.dram_tensor("L0_attn_ln_beta", [D], F32, kind="ExternalInput").ap()
    dwin = nc.dram_tensor("L0_attn_W_in", [D, 8], F32, kind="ExternalInput").ap()
    dbin = nc.dram_tensor("L0_attn_b_in", [8], F32, kind="ExternalInput").ap()
    dlog = nc.dram_tensor("L0_attn_decay_logit", [8], F32, kind="ExternalInput").ap()
    dwout = nc.dram_tensor("L0_attn_W_out", [8, D], F32, kind="ExternalInput").ap()
    dbout = nc.dram_tensor("L0_attn_b_out", [D], F32, kind="ExternalInput").ap()
    dskp = nc.dram_tensor("L0_attn_D_skip", [D], F32, kind="ExternalInput").ap()
    dwdn = nc.dram_tensor("L0_attn_W_down", [D, 32], F32, kind="ExternalInput").ap()
    dbdn = nc.dram_tensor("L0_attn_b_down", [32], F32, kind="ExternalInput").ap()
    dwup = nc.dram_tensor("L0_attn_W_up", [32, D], F32, kind="ExternalInput").ap()
    dbup = nc.dram_tensor("L0_attn_b_up", [D], F32, kind="ExternalInput").ap()
    dker = nc.dram_tensor("L0_ffn_kernel", [D, D], F32, kind="ExternalInput").ap()
    dbia = nc.dram_tensor("L0_ffn_bias", [D], F32, kind="ExternalInput").ap()
    deyef = nc.dram_tensor("c_eye_f", [128, 128], F32, kind="ExternalInput").ap()
    deyeb = nc.dram_tensor("c_eye_b", [128, 128], BF16, kind="ExternalInput").ap()
    dgamc = nc.dram_tensor("c_gamma_ch", [128, ND], F32, kind="ExternalInput").ap()
    dbetc = nc.dram_tensor("c_beta_ch", [128, ND], F32, kind="ExternalInput").ap()
    dskpc = nc.dram_tensor("c_dskip_ch", [128, ND], F32, kind="ExternalInput").ap()
    dboutc = nc.dram_tensor("c_bout_ch", [128, ND], F32, kind="ExternalInput").ap()
    dout = nc.dram_tensor("out", [n_seq, seq_len, D], F32, kind="ExternalOutput").ap()

    P = Prog()
    es = ExitStack()

    def sb(name, shape, dt):
        return es.enter_context(nc.sbuf_tensor(name, shape, dt))

    def pst(name, shape, dt):
        return es.enter_context(nc.psum_tensor(name, shape, dt))

    with es:
        kq = sb("kq", [128, ND * D], BF16)
        wcat = sb("wcat", [128, ND * 96], BF16)
        woutaug = sb("woutaug", [9, D], BF16)
        wupaug = sb("wupaug", [33, D], BF16)
        wowdg = sb("wowdg", [8, 32], BF16)
        outer_rhs = sb("outer_rhs", [2, D], BF16)
        bqrow = sb("bqrow", [1, D], BF16)
        c1row_bf = sb("c1row_bf", [1, D], BF16)
        swA96 = sb("swA96", [1, 96], BF16)
        eyef = sb("eyef", [128, 128], F32)
        eyeb = sb("eyeb", [128, 128], BF16)
        onesrow_b = sb("onesrow_b", [1, 128], BF16)
        ones32row = sb("ones32row", [1, 32], BF16)
        onescol_b = sb("onescol_b", [128, 1], BF16)
        csrow = sb("csrow", [1, 32], BF16)
        c8 = sb("c8", [8, 1], F32)
        cB = sb("cB", [32, 1], F32)
        cS_col = sb("cS_col", [32, 1], F32)
        decay = sb("decay", [8, 1], F32)
        decay_rep = sb("decay_rep", [8, CH], F32)
        gamc = sb("gamc", [128, ND], F32)
        betc = sb("betc", [128, ND], F32)
        skpc = sb("skpc", [128, ND], F32)
        boutc = sb("boutc", [128, ND], F32)
        woutT = sb("woutT", [128, ND * 8], BF16)
        gdc = sb("gdc", [128, ND], F32)
        bdc = sb("bdc", [128, ND], F32)
        c1c_bf = sb("c1c_bf", [128, ND], BF16)
        logit = sb("logit", [8, 1], F32)
        bin_t = sb("bin_t", [8, 1], F32)
        bdn_t = sb("bdn_t", [32, 1], F32)
        scal1 = sb("scal1", [8, 4], F32)
        cq128 = sb("cq128", [128, 1], F32)
        cm128 = sb("cm128", [128, 1], F32)
        bsc = sb("bsc", [1, 8], F32)
        kstat = sb("kstat", [128, 8], F32)
        wint = sb("wint", [128, ND * 8], F32)
        wdnt = sb("wdnt", [128, ND * 32], F32)
        x_t = [sb(f"x{i}", [128, D], F32) for i in range(2)]
        xn_t = [sb(f"xn{i}", [128, D], F32) for i in range(2)]
        xnb_t = sb("xnb", [128, D], BF16)
        u8_t = sb("u8", [128, D], U8)
        qbf_t = sb("qbf", [128, D], BF16)
        ixqT = [sb(f"ixqT{i}", [128, D], BF16) for i in range(2)]
        augrow = [sb(f"augrow{i}", [1, 128], BF16) for i in range(2)]
        xnT = sb("xnT", [128, D], BF16)
        s_t = [sb(f"s{i}", [128, D], F32) for i in range(2)]
        x2_t = sb("x2", [128, D], F32)
        x2tmp = sb("x2tmp", [128, 512], F32)
        sqscr = sb("sqscr", [128, 512], F32)
        t1_t = sb("t1", [128, D], F32)
        qi_t = sb("qi", [128, 1024], I32)
        r_t = sb("r", [128, 1024], BF16)
        g_t = sb("g", [128, 1024], F32)
        o_t = [sb(f"o{i}", [128, 1024], F32) for i in range(2)]
        t24 = [sb(f"t24_{i}", [128, 24], F32) for i in range(2)]
        st_t = [sb(f"st{i}", [128, 32], F32) for i in range(2)]
        nw_t = [sb(f"nw{i}", [128, 8], F32) for i in range(2)]
        nwi_t = [sb(f"nwi{i}", [128, 4], I32) for i in range(2)]
        u_sb = sb("u_sb", [32, CH], F32)
        uA_sb = sb("uA_sb", [32, CH], F32)
        uB_sb = sb("uB_sb", [32, CH], F32)
        hfv = sb("hfv", [32, CH], F32)
        h_t = [sb(f"h{i}", [8, CH], F32) for i in range(2)]
        haug = sb("haug", [9, CH], BF16)
        hfaug = sb("hfaug", [33, CH], BF16)
        r2row = sb("r2row", [1, 128], BF16)
        rm2row = sb("rm2row", [1, 128], BF16)
        r1aug = sb("r1aug", [2, 128], BF16)
        row_beta = xn_t[0][0:1, :]
        row_bout = x2_t[0:1, :]
        row_dskip = t1_t[0:1, :]
        row_gamma = s_t[0][0:1, :]
        wout_f = x_t[0][0:8, :]
        wup_f = x_t[1][0:32, :]
        kcol = [xn_t[1], s_t[1]]
        bqu8 = u8_t[0:1, :]
        ku8 = u8_t
        kT = [qbf_t, ixqT[0]]

        py = [pst(f"py{i}", [128, 512], F32) for i in range(2)]
        pb2 = pst("pb2", [128, 512], F32)
        pb3 = pst("pb3", [128, 512], F32)
        pb4 = pst("pb4", [128, 512], F32)
        pff = [pst(f"pff{i}", [128, 512], F32) for i in range(3)]
        prep1 = pff[0][0:32, 0:128]
        prep2 = pff[0][0:32, 128:256]
        prow = pff[1][0:1, 0:128]
        pT2 = pff[2][0:32, 0:128]
        pw8 = py[0][0:8, 0:32]
        pw32 = py[0][0:32, 32:33]
        pw1 = py[1][0:1, 0:32]
        pw128 = py[1][0:128, 8:16]
        pbwd = pb4

        sems = {}
        for e in Prog.ENGS:
            sems[e] = es.enter_context(nc.semaphore(f"sem_{e}"))
        for dname in ("d_w", "d_w2", "d_x0", "d_x1", "d_o0", "d_o1", "d_k0", "d_k1", "d_t"):
            sems[dname] = es.enter_context(nc.semaphore(dname))

        def dma(sem, deps, out, in_):
            return P.op("sync", deps, lambda e, o=out, i=in_: e.dma_start(out=o, in_=i),
                        dma_sem=sem)

        def T(deps, fn, inc=False):
            return P.op("tensor", deps, fn, inc=inc)

        def V(deps, fn, inc=True):
            return P.op("vector", deps, fn, inc=inc)

        def S(deps, fn, inc=True):
            return P.op("scalar", deps, fn, inc=inc)

        def G(deps, fn, inc=True):
            return P.op("gpsimd", deps, fn, inc=inc)

        L = {}
        L["eyef"] = dma("d_w", [], eyef[:], deyef[:])
        L["eyeb"] = dma("d_w", [], eyeb[:], deyeb[:])
        L["gamc"] = dma("d_w", [], gamc[:], dgamc[:])
        L["betc"] = dma("d_w", [], betc[:], dbetc[:])
        L["skpc"] = dma("d_w", [], skpc[:], dskpc[:])
        L["boutc"] = dma("d_w", [], boutc[:], dboutc[:])
        for j in range(ND):
            L[f"win{j}"] = dma("d_w", [], wint[:, 8 * j:8 * j + 8], dwin[128 * j:128 * (j + 1), :])
            L[f"wdn{j}"] = dma("d_w", [], wdnt[:, 32 * j:32 * j + 32], dwdn[128 * j:128 * (j + 1), :])
        L["wout"] = dma("d_w", [], wout_f, dwout[:])
        L["wup"] = dma("d_w", [], wup_f, dwup[:])
        L["brow"] = dma("d_w", [], row_beta, dbet.rearrange("(a b) -> a b", a=1))
        L["borow"] = dma("d_w", [], row_bout, dbout.rearrange("(a b) -> a b", a=1))
        L["dsrow"] = dma("d_w", [], row_dskip, dskp.rearrange("(a b) -> a b", a=1))
        L["garow"] = dma("d_w", [], row_gamma, dgam.rearrange("(a b) -> a b", a=1))
        L["logit"] = dma("d_w", [], logit[:], dlog.rearrange("(a b) -> a b", b=1))
        L["bin"] = dma("d_w", [], bin_t[:], dbin.rearrange("(a b) -> a b", b=1))
        L["bdn"] = dma("d_w", [], bdn_t[:], dbdn.rearrange("(a b) -> a b", b=1))

        c_ones1 = V([], lambda e: e.memset(onesrow_b[:], 1.0))
        c_o32 = V([], lambda e: e.memset(ones32row[:], 1.0))
        c_oc = V([], lambda e: e.memset(onescol_b[:], 1.0))
        c_r1a = V([], lambda e: e.memset(r1aug[:], 1.0))
        c_haug = V([], lambda e: e.memset(haug[:], 1.0))
        c_hfau = V([], lambda e: e.memset(hfaug[32:33, :], 1.0))
        c_q128 = V([], lambda e: e.memset(cq128[:], 128.0))
        c_sw0 = V([], lambda e: e.memset(swA96[:], 0.0))
        c_m128 = V([], lambda e: e.memset(cm128[:], -128.0))
        c_wz = V([], lambda e: e.memset(wcat[:], 0.0))

        p_e = S([L["logit"]], lambda e: e.activation(scal1[:, 0:1], logit[:], AF.Exp, bias=0.0, scale=-1.0))
        p_e2 = V([p_e], lambda e: e.tensor_single_scalar(scal1[:, 1:2], scal1[:, 0:1], 1.0, OP.add))
        p_dec = V([p_e2], lambda e: e.reciprocal(decay[:], scal1[:, 1:2]))
        p_dr1 = V([], lambda e: e.memset(decay_rep[:], 1.0))
        p_dr2 = V([p_dr1, p_dec], lambda e: e.tensor_scalar(decay_rep[:], decay_rep[:], decay[:], None, OP.mult))

        p_gdr = V([L["garow"], L["dsrow"]], lambda e: e.tensor_tensor(row_gamma, row_gamma, row_dskip, OP.mult))
        p_bdr = V([L["brow"], L["dsrow"]], lambda e: e.tensor_tensor(row_beta, row_beta, row_dskip, OP.mult))
        p_c1r = V([p_bdr, L["borow"]], lambda e: e.tensor_tensor(row_bout, row_bout, row_beta, OP.add))
        p_orhs1 = V([], lambda e: e.memset(outer_rhs[:], 1.0))
        p_orhs0 = V([p_gdr, p_orhs1], lambda e: e.tensor_copy(outer_rhs[0:1, :], row_gamma))
        L["burow"] = dma("d_w2", [p_gdr, p_bdr], row_dskip, dbup.rearrange("(a b) -> a b", a=1))
        L["birow"] = dma("d_w2", [p_c1r], row_beta, dbia.rearrange("(a b) -> a b", a=1))
        p_wo = V([L["wout"]], lambda e: e.tensor_copy(woutaug[0:8, :], wout_f))
        p_wo1v = V([p_c1r], lambda e: e.tensor_copy(c1row_bf[:], row_bout))
        p_wo1 = dma("d_w2", [p_wo1v], woutaug[8:9, :], c1row_bf[:])
        p_wu = V([L["wup"]], lambda e: e.tensor_copy(wupaug[0:32, :], wup_f))
        p_wu1 = V([L["burow"]], lambda e: e.tensor_copy(wupaug[32:33, :], row_dskip))
        p_gdc = V([L["gamc"], L["skpc"]], lambda e: e.tensor_tensor(gdc[:], gamc[:], skpc[:], OP.mult))
        p_bdc = V([L["betc"], L["skpc"]], lambda e: e.tensor_tensor(bdc[:], betc[:], skpc[:], OP.mult))
        p_c1c = V([p_bdc, L["boutc"]], lambda e: e.tensor_tensor(bdc[:], boutc[:], bdc[:], OP.add))
        p_c1cb = V([p_c1c], lambda e: e.tensor_copy(c1c_bf[:], bdc[:]))
        p_wcat = []
        for j in range(ND):
            p_wcat.append(V([L[f"win{j}"], L["gamc"], c_wz],
                            lambda e, j=j: e.tensor_scalar(wcat[:, 96 * j:96 * j + 8], wint[:, 8 * j:8 * j + 8], gamc[:, j:j + 1], None, OP.mult)))
            p_wcat.append(V([L[f"wdn{j}"], L["gamc"], c_wz],
                            lambda e, j=j: e.tensor_scalar(wcat[:, 96 * j + 32:96 * j + 64], wdnt[:, 32 * j:32 * j + 32], gamc[:, j:j + 1], None, OP.mult)))
            p_wcat.append(V([L[f"wdn{j}"], p_gdc, c_wz],
                            lambda e, j=j: e.tensor_scalar(wcat[:, 96 * j + 64:96 * j + 96], wdnt[:, 32 * j:32 * j + 32], gdc[:, j:j + 1], None, OP.mult)))

        mm = None
        for j in range(ND):
            mm = T([L[f"win{j}"], L["betc"], mm],
                   lambda e, j=j: e.matmul(pw8[0:8, 0:1], wint[:, 8 * j:8 * j + 8], betc[:, j:j + 1],
                                           start=(j == 0), stop=(j == ND - 1), skip_group_check=True), inc=(j == ND - 1))
        p_c8 = V([mm, L["bin"]], lambda e: e.tensor_tensor(c8[:], pw8[0:8, 0:1], bin_t[:], OP.add))
        mm = None
        for j in range(ND):
            mm = T([L[f"wdn{j}"], L["betc"], p_c8 if j == 0 else mm],
                   lambda e, j=j: e.matmul(pw32[:], wdnt[:, 32 * j:32 * j + 32], betc[:, j:j + 1],
                                           start=(j == 0), stop=(j == ND - 1), skip_group_check=True), inc=(j == ND - 1))
        p_cB = V([mm, L["bdn"]], lambda e: e.tensor_tensor(cB[:], pw32[:], bdn_t[:], OP.add))
        mm = None
        for j in range(ND):
            mm = T([p_wcat[3 * j + 1], c_oc, p_cB if j == 0 else mm],
                   lambda e, j=j: e.matmul(pw32[:], wcat[:, 96 * j + 32:96 * j + 64], onescol_b[:],
                                           start=(j == 0), stop=(j == ND - 1), skip_group_check=True), inc=(j == ND - 1))
        p_cSv = V([mm], lambda e: e.tensor_copy(cS_col[:], pw32[:]))
        p_cSr = T([p_cSv, L["eyef"]], lambda e: e.transpose(pw1[:], cS_col[:], eyef[0:32, 0:32]), inc=True)
        p_cSe = V([p_cSr], lambda e: e.tensor_copy(csrow[:], pw1[:]))
        p_wT = []
        prev = p_cSe
        for j in range(ND):
            t = T([L["wout"], L["eyef"], prev], lambda e, j=j: e.transpose(pw128[:], wout_f[:, 128 * j:128 * (j + 1)], eyef[0:8, 0:8]), inc=True)
            prev = V([t], lambda e, j=j: e.tensor_copy(woutT[:, 8 * j:8 * j + 8], pw128[:]))
            p_wT.append(prev)
        mm = None
        for j in range(ND):
            mm = T([p_wT[j], p_wcat[3 * j + 1], p_cSr if j == 0 else mm],
                   lambda e, j=j: e.matmul(pw8[:], woutT[:, 8 * j:8 * j + 8], wcat[:, 96 * j + 32:96 * j + 64],
                                           start=(j == 0), stop=(j == ND - 1), skip_group_check=True), inc=(j == ND - 1))
        p_wow = V([mm], lambda e: e.tensor_copy(wowdg[:], pw8[:]))
        mm = None
        for j in range(ND):
            mm = T([p_c1cb, p_wcat[3 * j + 1], p_wT[-1] if j == 0 else mm],
                   lambda e, j=j: e.matmul(pw1[:], c1c_bf[:, j:j + 1], wcat[:, 96 * j + 32:96 * j + 64],
                                           start=(j == 0), stop=(j == ND - 1), skip_group_check=True), inc=(j == ND - 1))
        p_cw2 = V([mm, c_sw0], lambda e: e.tensor_copy(swA96[0:1, 32:64], pw1[:]))
        p_bam = V([L["birow"]], lambda e: e.reduce_max(bsc[:, 0:1], row_beta, axis=AX, apply_absolute_value=True))
        p_bg = V([p_bam], lambda e: e.tensor_single_scalar(bsc[:, 1:2], bsc[:, 0:1], 1e-30, OP.max))
        p_bre = V([p_bg], lambda e: e.reciprocal(bsc[:, 2:3], bsc[:, 1:2]))
        p_binv = V([p_bre], lambda e: e.tensor_single_scalar(bsc[:, 3:4], bsc[:, 2:3], 127.0, OP.mult))
        p_bsc = V([p_bg], lambda e: e.tensor_single_scalar(bsc[:, 4:5], bsc[:, 1:2], 1.0 / 127.0, OP.mult))
        p_bnb = V([p_bsc], lambda e: e.tensor_single_scalar(bsc[:, 5:6], bsc[:, 4:5], -128.0, OP.mult))
        p_bq1 = S([p_binv, L["birow"], c_q128], lambda e: e.activation(bqu8[:], row_beta, AF.Identity, bias=cq128[0:1, :], scale=bsc[:, 3:4]))
        p_bq2 = S([p_bq1, p_bnb], lambda e: e.activation(bqrow[:], bqu8[:], AF.Identity, bias=bsc[:, 5:6], scale=bsc[:, 4:5]))

        kqr = kq[:].rearrange("p (dj f) -> p dj f", dj=ND)
        hist = []
        for fi in range(ND):
            h = {}
            buf = kcol[fi % 2]
            prev_q0 = hist[fi - 1]["q2"] if fi >= 1 else p_bq2
            prev_k6 = hist[fi - 1]["k6"] if fi >= 1 else None
            loads = []
            for djg in range(ND):
                dep_prev = hist[fi - 2]["bwd_last"] if fi >= 2 else None
                dep_first = p_wcat[-1] if (fi <= 1 and djg == 0) else None
                loads.append(dma(f"d_k{fi % 2}", [dep_prev, dep_first],
                                 buf[:, 128 * djg:128 * (djg + 1)],
                                 dker[128 * djg:128 * (djg + 1), 128 * fi:128 * (fi + 1)]))
            egrp = []
            tlast = None
            for g in range(4):
                tt = None
                for b in range(4):
                    dj = 4 * g + b
                    deps = [loads[-1], L["eyef"]]
                    if b == 0:
                        if g == 0:
                            deps.append(hist[fi - 1]["fwd_evac"][3] if fi >= 1 else None)
                        else:
                            deps.append(egrp[g - 1])
                        if fi >= 2:
                            deps.append(hist[fi - 2]["q1"])
                    tt = T(deps, lambda e, dj=dj, b=b, buf=buf: e.transpose(
                        pb2[:, 128 * b:128 * (b + 1)], buf[:, 128 * dj:128 * (dj + 1)], eyef[:]),
                        inc=(b == 3))
                egrp.append(V([tt, prev_k6 if g == 0 else None], lambda e, g=g, fi=fi: e.tensor_copy(
                    kT[fi % 2][:, 512 * g:512 * (g + 1)], pb2[:])))
                tlast = tt
            h["fwd_evac"] = egrp
            h["fwd_last"] = tlast
            k1 = V([egrp[0], egrp[1], egrp[2], egrp[3], prev_q0, prev_k6],
                   lambda e, fi=fi: e.reduce_max(kstat[:, 4:5], kT[fi % 2][:], axis=AX, apply_absolute_value=True))
            k2 = V([k1], lambda e: e.tensor_single_scalar(kstat[:, 4:5], kstat[:, 4:5], 1e-30, OP.max))
            k3 = V([k2], lambda e: e.reciprocal(kstat[:, 5:6], kstat[:, 4:5]))
            k4 = V([k3], lambda e: e.tensor_single_scalar(kstat[:, 5:6], kstat[:, 5:6], 127.0, OP.mult))
            k5 = V([k2], lambda e: e.tensor_single_scalar(kstat[:, 6:7], kstat[:, 4:5], 1.0 / 127.0, OP.mult))
            k6 = V([k5], lambda e: e.tensor_single_scalar(kstat[:, 7:8], kstat[:, 6:7], -128.0, OP.mult))
            h["k6"] = k6
            q1 = S([k4, egrp[-1], prev_q0, c_q128],
                   lambda e, fi=fi: e.activation(ku8[:], kT[fi % 2][:], AF.Identity, bias=cq128[:], scale=kstat[:, 5:6]))
            h["q1"] = q1
            kqf = buf
            q2 = S([q1, k6, tlast], lambda e, kqf=kqf: e.activation(kqf[:], ku8[:], AF.Identity, bias=kstat[:, 7:8], scale=kstat[:, 6:7]))
            h["q2"] = q2
            bgrp = []
            blast = None
            for g in range(4):
                tt = None
                for b in range(4):
                    dj = 4 * g + b
                    deps = [q2, L["eyef"]]
                    if b == 0:
                        if g == 0:
                            deps.append(hist[fi - 1]["bwd_evac"][3] if fi >= 1 else None)
                        else:
                            deps.append(bgrp[g - 1])
                    tt = T(deps, lambda e, dj=dj, b=b, kqf=kqf: e.transpose(
                        pbwd[:, 128 * b:128 * (b + 1)], kqf[:, 128 * dj:128 * (dj + 1)], eyef[:]),
                        inc=(b == 3))
                ev = V([tt], lambda e, g=g, fi=fi: e.tensor_copy(
                    kqr[:, 4 * g:4 * g + 4, 128 * fi:128 * (fi + 1)],
                    pbwd[:].rearrange("p (a b) -> p a b", a=4)))
                bgrp.append(ev)
                blast = tt
            h["bwd_evac"] = bgrp
            h["bwd_last"] = blast
            hist.append(h)
        KQ_READY = hist[-1]["bwd_evac"][-1]
        KT0_FREE = hist[14]["q2"]
        KT1_FREE = hist[15]["q2"]
        KCOL0_FREE = hist[14]["bwd_last"]
        KCOL1_FREE = hist[15]["bwd_last"]

        R = [dict() for _ in range(NCH)]

        def emit_pro(c):
            rc = R[c]
            seq, t0 = c // CPS, (c % CPS) * CH
            xb = x_t[c % 2]
            xn = xn_t[c % 2]
            xdeps = []
            if c >= 2:
                xdeps += [R[c - 2].get("o1"), R[c - 2].get("s1"), R[c - 2].get("e2b")]
            if c == 0:
                xdeps += [p_wo, p_wT[-1]]
            if c == 1:
                xdeps += [p_wu]
            rc["xld"] = dma(f"d_x{c % 2}", xdeps, xb[:], dx[seq, t0:t0 + CH, :])
            st = st_t[c % 2]
            tb = t24[c % 2]
            nw = nw_t[c % 2]
            nwi = nwi_t[c % 2]
            dep_st = R[c - 2].get("e4b") if c >= 2 else None
            dep_st2 = R[c - 2].get("e3_1") if c >= 2 else None
            bn = []
            for q in range(4):
                bn.append(V([rc["xld"], dep_st, dep_st2], lambda e, q=q: e.bn_stats(tb[:, 6 * q:6 * q + 6], xb[:, 512 * q:512 * (q + 1)])))
            ag = V([bn[0], bn[1], bn[2], bn[3]], lambda e: e.bn_aggr(st[:, 0:2], tb[:]))
            c1_ = V([ag], lambda e: e.tensor_single_scalar(nw[:, 0:1], st[:, 1:2], EPS, OP.add))
            c2_ = V([c1_], lambda e: e.reciprocal(st[:, 5:6], nw[:, 0:1]))
            c3_ = V([c2_], lambda e: e.tensor_tensor(st[:, 6:7], st[:, 1:2], st[:, 5:6], OP.mult))
            c4_ = V([c3_], lambda e: e.tensor_single_scalar(nw[:, 1:2], st[:, 6:7], EPS, OP.add))
            rcp = V([c4_], lambda e: e.reciprocal(nw[:, 4:6], nw[:, 0:2]))
            nwd = S([rcp], lambda e: e.activation(st[:, 7:9], nw[:, 4:6], AF.Sqrt, bias=0.0, scale=1.0))
            rc["st0"] = nwd
            x1 = V([nwd], lambda e: e.tensor_tensor(st[:, 9:10], st[:, 0:1], st[:, 7:8], OP.mult))
            x2b_ = V([x1], lambda e: e.tensor_single_scalar(st[:, 9:10], st[:, 9:10], -1.0, OP.mult))
            dep_xn1 = R[c - 2].get("x2a3") if c >= 2 else None
            dep_xn2 = R[c - 2].get("q8") if c >= 2 else None
            dep_xn3 = R[c - 2].get("xnbd") if c >= 2 else None
            if c == 0:
                dep_xn1 = p_bq2
            if c == 1:
                dep_xn2 = KCOL0_FREE
            rc["xn"] = S([x2b_, rc["xld"], dep_xn1, dep_xn2, dep_xn3],
                         lambda e, xn=xn: e.activation(xn[:], xb[:], AF.Identity, bias=st[:, 9:10], scale=st[:, 7:8]))
            if TRUNC == 21:
                rc["last"] = [rc["xn"]]
                return
            dep_xnb = R[c - 1].get("t1b") if c >= 1 else None
            rc["xnbd"] = S([rc["xn"], dep_xnb], lambda e, xn=xn: e.activation(xnb_t[:], xn[:], AF.Identity, bias=0.0, scale=1.0))
            rc["am"] = V([rc["xn"]], lambda e, xn=xn: e.reduce_max(st[:, 10:11], xn[:], axis=AX, apply_absolute_value=True))
            v1 = V([rc["am"]], lambda e: e.tensor_single_scalar(st[:, 10:11], st[:, 10:11], 1e-30, OP.max))
            v2 = V([v1], lambda e: e.reciprocal(st[:, 11:12], st[:, 10:11]))
            v3 = V([v2], lambda e: e.tensor_single_scalar(st[:, 11:12], st[:, 11:12], 127.0, OP.mult))
            v4 = V([v1], lambda e: e.tensor_single_scalar(st[:, 12:13], st[:, 10:11], 1.0 / 127.0, OP.mult))
            rc["inv0"] = v3
            rc["sc0"] = v4
            dep_u8 = R[c - 1].get("qbf") if c >= 1 else p_bq2
            rc["q8"] = S([v3, dep_u8, c_q128], lambda e, xn=xn: e.activation(u8_t[:], xn[:], AF.Identity, bias=cq128[:], scale=st[:, 11:12]))
            dep_qbf = R[c - 1].get("t2b") if c >= 1 else KT0_FREE
            rc["qbf"] = S([rc["q8"], dep_qbf, c_m128], lambda e: e.activation(qbf_t[:], u8_t[:], AF.Identity, bias=cm128[:], scale=1.0))
            return

        def emit_mid(c):
            rc = R[c]
            st = st_t[c % 2]
            nw = nw_t[c % 2]
            nwi = nwi_t[c % 2]
            xn = xn_t[c % 2]
            nwd = rc["st0"]
            dep_prow = R[c - 1].get("rm2e") if c >= 1 else None
            dep_prow2 = R[c - 1].get("s1") if c >= 1 else None
            rc["augT"] = T([rc["inv0"], dep_prow, dep_prow2, L["eyef"]],
                           lambda e: e.transpose(prow[:], st[:, 11:12], eyef[:]), inc=True)
            dep_aug_free = R[c - 2].get("bigb") if c >= 2 else None
            rc["auge"] = S([rc["augT"], dep_aug_free],
                           lambda e: e.activation(augrow[c % 2][:], prow[:], AF.Identity, bias=0.0, scale=1.0))
            rc["r1T"] = T([nwd, rc["auge"], L["eyef"]],
                          lambda e: e.transpose(prow[:], st[:, 8:9], eyef[:]), inc=True)
            rc["r1a"] = V([rc["r1T"], c_r1a], lambda e: e.tensor_copy(r1aug[0:1, :], prow[:]))
            if TRUNC == 221:
                rc["last"] = [rc["r1a"], rc["auge"]]
                return
            dep_prep = R[c - 1].get("hf1") if c >= 1 else None
            rc["rep1"] = T([rc["r1a"], c_o32, dep_prep],
                           lambda e: e.matmul(prep1[:], ones32row[:], r1aug[0:1, :], start=True, stop=True, skip_group_check=True), inc=True)
            if TRUNC == 222:
                rc["rep1e"] = V([rc["rep1"]], lambda e: e.tensor_copy(u_sb[0:8, :], prep1[0:8, :]))
                rc["last"] = [rc["rep1e"]]
                return
            tb2 = [pb2, pb4]
            dep_pT0 = R[c - 1].get("t2e2") if c >= 1 else KQ_READY
            dep_pT1 = [R[c - 1].get("t2e3"), R[c - 1].get("s1")] if c >= 1 else [KQ_READY]
            dep_xnT_free = R[c - 1].get("swB") if c >= 1 else None
            t1e = []
            tlast = None
            for g in range(4):
                bank = tb2[g % 2]
                for b in range(4):
                    j = 4 * g + b
                    deps = [rc["xnbd"], L["eyeb"]]
                    if b == 0:
                        if g == 0:
                            deps.append(dep_pT0)
                        elif g == 1:
                            deps += dep_pT1
                        else:
                            deps.append(t1e[g - 2])
                    tlast = T(deps, lambda e, j=j, b=b, bank=bank: e.matmul(
                        bank[:, 128 * b:128 * (b + 1)], xnb_t[:, 128 * j:128 * (j + 1)], eyeb[:],
                        start=True, stop=True, skip_group_check=True), inc=(b == 3))
                t1e.append(V([tlast, dep_xnT_free],
                             lambda e, g=g, bank=bank: e.tensor_copy(xnT[:, 512 * g:512 * (g + 1)], bank[:])))
            rc["t1b"] = tlast
            ixb = ixqT[c % 2]
            dep_ix_free = R[c - 2].get("bigb") if c >= 2 else (KT1_FREE if c == 0 else None)
            t2e = []
            tlast = None
            for g in range(4):
                bank = tb2[g % 2]
                for b in range(4):
                    j = 4 * g + b
                    deps = [rc["qbf"], L["eyeb"]]
                    if b == 0:
                        deps.append(t1e[g + 2] if g < 2 else t2e[g - 2])
                    tlast = T(deps, lambda e, j=j, b=b, bank=bank: e.matmul(
                        bank[:, 128 * b:128 * (b + 1)], qbf_t[:, 128 * j:128 * (j + 1)], eyeb[:],
                        start=True, stop=True, skip_group_check=True), inc=(b == 3))
                t2e.append(V([tlast, dep_ix_free],
                             lambda e, g=g, ixb=ixb, bank=bank: e.tensor_copy(ixb[:, 512 * g:512 * (g + 1)], bank[:])))
            rc["t2b"] = tlast
            rc["t2e2"] = t2e[2]
            rc["t2e3"] = t2e[3]
            rc["t2e"] = t2e
            rc["ixb"] = ixb
            if TRUNC == 22:
                rc["last"] = [t2e[3], t1e[3]]
                return
            dep_u_free = R[c - 1].get("uBe") if c >= 1 else None
            swu = None
            for j in range(ND):
                deps = [t1e[j // 4], p_wcat[3 * j], p_wcat[3 * j + 1], p_wcat[3 * j + 2]]
                if j == 0:
                    deps.append(dep_u_free)
                swu = T(deps, lambda e, j=j: e.matmul(
                    pb3[0:96, 0:128], wcat[:, 96 * j:96 * (j + 1)], xnT[:, 128 * j:128 * (j + 1)],
                    start=(j == 0), stop=False, skip_group_check=True), inc=False)
            swu = T([swu, c_ones1, p_cw2], lambda e: e.matmul(
                pb3[0:96, 0:128], swA96[:], onesrow_b[:], start=False, stop=False, skip_group_check=True), inc=True)
            rc["swB"] = swu
            rc["ue"] = V([swu], lambda e: e.tensor_copy(u_sb[:], pb3[0:32, 0:128]))
            um = V([rc["ue"], rc["rep1"]], lambda e: e.tensor_tensor(u_sb[0:8, :], u_sb[0:8, :], prep1[0:8, :], OP.mult))
            uc = V([um, p_c8], lambda e: e.tensor_scalar(u_sb[0:8, :], u_sb[0:8, :], c8[:], None, OP.add))
            hb_ = h_t[c % 2]
            init = 0.0 if c % CPS == 0 else h_t[(c - 1) % 2][:, CH - 1:CH]
            dep_h = R[c - 2].get("scan") if c >= 2 else None
            rc["scan"] = V([uc, p_dr2, dep_h], lambda e, hb_=hb_: e.tensor_tensor_scan(hb_[:], decay_rep[:], u_sb[0:8, :], init, OP.mult, OP.add))
            dep_ha = R[c - 1].get("xarlast") if c >= 1 else None
            dep_ha2 = R[c - 1].get("wow") if c >= 1 else None
            rc["hb"] = V([rc["scan"], dep_ha, dep_ha2, c_haug], lambda e, hb_=hb_: e.tensor_copy(haug[0:8, :], hb_[:]))
            rc["wow"] = T([rc["hb"], p_wow, swu], lambda e: e.matmul(
                pb3[32:64, 0:128], wowdg[:], haug[0:8, :], start=False, stop=True, skip_group_check=True), inc=True)
            rc["uAe"] = V([rc["wow"]], lambda e: e.tensor_copy(uA_sb[:], pb3[32:64, 0:128]))
            rc["uBe"] = V([rc["uAe"]], lambda e: e.tensor_copy(uB_sb[:], pb3[64:96, 0:128]))
            if TRUNC == 23:
                rc["last"] = [rc["uBe"], rc["auge"]]
                return
            e1prev = R[c - 1].get("e1list") if c >= 1 else None
            prev_a = prev_b = None
            for q in range(NQ):
                deps_o = [rc["r1a"], p_orhs0, t2e[3]]
                if q == 0:
                    deps_o.append(e1prev[2] if e1prev else (KQ_READY if c == 0 else None))
                else:
                    deps_o.append(prev_a)
                mo = T(deps_o, lambda e, q=q: e.matmul(py[0][:], r1aug[:], outer_rhs[:, 512 * q:512 * (q + 1)], start=True, stop=True, skip_group_check=True), inc=True)
                deps_x = [rc["hb"], p_wo, p_wo1]
                if q == 0:
                    deps_x.append(e1prev[3] if e1prev else None)
                else:
                    deps_x.append(prev_b)
                mx = T(deps_x, lambda e, q=q: e.matmul(py[1][:], haug[:], woutaug[:, 512 * q:512 * (q + 1)], start=True, stop=True, skip_group_check=True), inc=True)
                va = V([mo, rc["xn"], prev_b], lambda e, q=q, xn=xn: e.tensor_tensor(x2tmp[:], xn[:, 512 * q:512 * (q + 1)], py[0][:], OP.mult))
                vb = V([va, mx], lambda e, q=q: e.scalar_tensor_tensor(
                    x2_t[:, 512 * q:512 * (q + 1)], x2tmp[:], 1.0, py[1][:], OP.mult, OP.add,
                    accum_out=st[:, 13 + q:14 + q]))
                prev_a, prev_b = va, vb
                rc[f"x2a{q}"] = va
                rc[f"x2b{q}"] = vb
                rc["xarlast"] = mx
            return

        def emit_mid2(c):
            rc = R[c]
            st = st_t[c % 2]
            nw = nw_t[c % 2]
            nwi = nwi_t[c % 2]
            if TRUNC == 24:
                rc["last"] = [rc["x2b3"], rc["x2a3"]]
                return
            sq = []
            for q in range(4):
                deps = [rc[f"x2b{q}"]]
                if q == 0:
                    deps.append(R[c - 1].get("e3_1") if c >= 1 else KCOL1_FREE)
                else:
                    deps.append(sq[q - 1])
                sq.append(S(deps, lambda e, q=q: e.activation(sqscr[:], x2_t[:, 512 * q:512 * (q + 1)], AF.Square,
                                                              bias=0.0, scale=1.0, accum_out=st[:, 17 + q:18 + q])))
            rc["sq1"] = sq[3]
            s1_ = V([rc["x2b3"]], lambda e: e.reduce_sum(st[:, 21:22], st[:, 13:17], axis=AX))
            s2_ = V([s1_], lambda e: e.tensor_single_scalar(st[:, 21:22], st[:, 21:22], 1.0 / D, OP.mult))
            s3_ = V([sq[3]], lambda e: e.reduce_sum(st[:, 22:23], st[:, 17:21], axis=AX))
            s4_ = V([s3_], lambda e: e.tensor_single_scalar(st[:, 22:23], st[:, 22:23], 1.0 / D, OP.mult))
            s5_ = V([s2_], lambda e: e.tensor_tensor(st[:, 23:24], st[:, 21:22], st[:, 21:22], OP.mult))
            s6_ = V([s4_, s5_], lambda e: e.tensor_tensor(st[:, 22:23], st[:, 22:23], st[:, 23:24], OP.subtract))
            s7_ = V([s6_], lambda e: e.tensor_single_scalar(nw[:, 6:7], st[:, 22:23], EPS, OP.add))
            rcp2 = V([s7_], lambda e: e.reciprocal(nw[:, 7:8], nw[:, 6:7]))
            s8_ = S([rcp2], lambda e: e.activation(st[:, 24:25], nw[:, 7:8], AF.Sqrt, bias=0.0, scale=1.0))
            s9_ = V([s8_], lambda e: e.tensor_tensor(st[:, 25:26], st[:, 24:25], st[:, 21:22], OP.mult))
            rT1 = T([s8_, rc["r1T"], L["eyef"]], lambda e: e.transpose(prow[:], st[:, 24:25], eyef[:]), inc=True)
            rc["r2e"] = V([rT1], lambda e: e.tensor_copy(r2row[:], prow[:]))
            rT2 = T([s9_, rc["r2e"], L["eyef"]], lambda e: e.transpose(prow[:], st[:, 25:26], eyef[:]), inc=True)
            rc["rm2e"] = V([rT2], lambda e: e.tensor_copy(rm2row[:], prow[:]))
            dep_pr2 = R[c - 1].get("hf1") if c >= 1 else None
            rep2 = T([rc["r2e"], c_o32, dep_pr2], lambda e: e.matmul(prep2[:], ones32row[:], r2row[:], start=True, stop=True, skip_group_check=True), inc=True)
            dep_pT2 = R[c - 1].get("hf1") if c >= 1 else None
            repT2 = T([rc["rm2e"], p_cSe, dep_pT2], lambda e: e.matmul(pT2[:], csrow[:], rm2row[:], start=True, stop=True, skip_group_check=True), inc=True)
            f1 = V([rc["uBe"], rc["rep1"], rep2], lambda e: e.tensor_tensor(hfv[:], uB_sb[:], prep1[:], OP.mult))
            f2 = V([f1, rc["uAe"]], lambda e: e.tensor_tensor(hfv[:], hfv[:], uA_sb[:], OP.add))
            f3 = V([f2, rep2], lambda e: e.tensor_tensor(hfv[:], hfv[:], prep2[:], OP.mult))
            f4 = V([f3, repT2, p_cB], lambda e: e.scalar_tensor_tensor(hfv[:], hfv[:], cB[:], pT2[:], OP.add, OP.subtract))
            dep_hfa = R[c - 1].get("wup3") if c >= 1 else None
            f5 = V([f4, dep_hfa, c_hfau], lambda e: e.tensor_scalar(hfaug[0:32, :], hfv[:], 0.0, None, OP.max))
            rc["hf1"] = f5
            if TRUNC == 25:
                rc["last"] = [f5]
                return
            prev_s = R[c - 1].get("s1") if c >= 1 else None
            wdst = [pb4, pff[0], pff[1], pff[2]]
            for q in range(NQ):
                deps = [f5, p_wu, p_wu1, prev_s]
                if q == 0:
                    deps += [rc["x2b3"], rc["t2e3"]]
                    if c == 0:
                        deps.append(KQ_READY)
                wu = T(deps, lambda e, q=q: e.matmul(
                    wdst[q][:], hfaug[:], wupaug[:, 512 * q:512 * (q + 1)], start=True, stop=True, skip_group_check=True), inc=True)
                rc[f"wup{q}"] = wu
            return

        def emit_slast(c):
            rc = R[c]
            sb_ = s_t[c % 2]
            wdst = [pb4, pff[0], pff[1], pff[2]]
            dep_s_free = R[c - 2].get("sg1") if c >= 2 else (KCOL1_FREE if c == 1 else None)
            slast = None
            for q in range(NQ):
                slast = V([rc[f"wup{q}"], dep_s_free, slast], lambda e, q=q: e.tensor_tensor(
                    sb_[:, 512 * q:512 * (q + 1)], x2_t[:, 512 * q:512 * (q + 1)], wdst[q][:], OP.add))
            rc["s1"] = slast

        def emit_big_half(c, hf_):
            rc = R[c]
            st = st_t[c % 2]
            t2e = rc["t2e"]
            ixb = rc["ixb"]
            dep_y0 = R[c - 1].get("e1b") if c >= 1 else KQ_READY
            dep_t1free = R[c - 1].get("e2b") if c >= 1 else None
            e1 = rc.setdefault("e1list", [])
            mmlast = None
            for j in range(ND):
                for qq in range(2):
                    fofs = 1024 * hf_ + 512 * qq
                    deps = [t2e[j // 4], KQ_READY]
                    if j == 0 and qq == 0:
                        deps.append(dep_y0 if hf_ == 0 else e1[-1])
                        deps += [rc["x2a3"], rc["x2b3"]]
                        if c == 0:
                            deps += [p_c8, p_cw2, p_cSe, p_wow]
                    mmlast = T(deps, lambda e, j=j, qq=qq, fofs=fofs, ixb=ixb: e.matmul(
                        py[qq][:], ixb[:, 128 * j:128 * (j + 1)],
                        kq[:, D * j + fofs:D * j + fofs + 512],
                        start=(j == 0), stop=False, skip_group_check=True), inc=False)
            for qq in range(2):
                fofs = 1024 * hf_ + 512 * qq
                mmlast = T([mmlast, rc["auge"], p_bq2], lambda e, qq=qq, fofs=fofs: e.matmul(
                    py[qq][:], augrow[c % 2][:], bqrow[:, fofs:fofs + 512],
                    start=False, stop=True, skip_group_check=True), inc=(qq == 1))
            rc[f"big{hf_}"] = mmlast

        def emit_big_evacs(c, hf_):
            rc = R[c]
            st = st_t[c % 2]
            dep_t1free = R[c - 1].get("e2b") if c >= 1 else None
            e1 = rc["e1list"]
            mmlast = rc[f"big{hf_}"]
            e1.append(V([mmlast, rc["sc0"], dep_t1free], lambda e, hf_=hf_: e.tensor_scalar(
                t1_t[:, 1024 * hf_:1024 * hf_ + 512], py[0][:], st[:, 12:13], None, OP.mult, OP.max,
                accum_out=st[:, 26 + 2 * hf_:27 + 2 * hf_])))
            e1.append(V([e1[-1]], lambda e, hf_=hf_: e.tensor_scalar(
                t1_t[:, 1024 * hf_ + 512:1024 * (hf_ + 1)], py[1][:], st[:, 12:13], None, OP.mult, OP.max,
                accum_out=st[:, 27 + 2 * hf_:28 + 2 * hf_])))
            if hf_ == 1:
                rc["bigb"] = rc["big1"]
                rc["e1b"] = e1[-1]

        def emit_echunk(c):
            rc = R[c]
            st = st_t[c % 2]
            xb = x_t[c % 2]
            sb_ = s_t[c % 2]
            seq_, t0_ = c // CPS, (c % CPS) * CH
            if TRUNC >= 21:
                for h in range(2):
                    dep_o_free = R[c - 1].get(f"store{h}") if c >= 1 else None
                    oo = V(rc["last"] + [dep_o_free], lambda e, h=h: e.tensor_copy(
                        o_t[h][:], xn_t[c % 2][:, 1024 * h:1024 * (h + 1)]))
                    rc[f"store{h}"] = dma(f"d_o{h}", [oo], dout[seq_, t0_:t0_ + CH, 1024 * h:1024 * (h + 1)], o_t[h][:])
                rc["store"] = rc["store1"]
                return
            if TRUNC == 2:
                for h in range(2):
                    dep_o_free = R[c - 1].get(f"store{h}") if c >= 1 else None
                    oo = V([rc["s1"], dep_o_free], lambda e, h=h: e.tensor_copy(
                        o_t[h][:], sb_[:, 1024 * h:1024 * (h + 1)]))
                    rc[f"sg{h}"] = oo
                    rc[f"o{h}"] = oo
                    rc[f"store{h}"] = dma(f"d_o{h}", [oo], dout[seq_, t0_:t0_ + CH, 1024 * h:1024 * (h + 1)], o_t[h][:])
                rc["store"] = rc["store1"]
                return
            if TRUNC == 3:
                e1_ = rc["e1list"]
                for h in range(2):
                    dep_o_free = R[c - 1].get(f"store{h}") if c >= 1 else None
                    oo = V([e1_[2 * h + 1], dep_o_free], lambda e, h=h: e.tensor_copy(
                        o_t[h][:], t1_t[:, 1024 * h:1024 * (h + 1)]))
                    rc[f"o{h}"] = oo
                    if h == 1:
                        rc["e2b"] = oo
                    rc[f"store{h}"] = dma(f"d_o{h}", [oo], dout[seq_, t0_:t0_ + CH, 1024 * h:1024 * (h + 1)], o_t[h][:])
                rc["store"] = rc["store1"]
                return
            e1 = rc["e1list"]
            g1 = V([e1[3]], lambda e: e.reduce_max(st[:, 30:31], st[:, 26:30], axis=AX))
            g2 = V([g1], lambda e: e.tensor_single_scalar(st[:, 30:31], st[:, 30:31], 0.0, OP.max))
            g3 = V([g2], lambda e: e.tensor_single_scalar(st[:, 30:31], st[:, 30:31], 1e-30, OP.max))
            g4 = V([g3], lambda e: e.reciprocal(st[:, 31:32], st[:, 30:31]))
            g5 = V([g4], lambda e: e.tensor_single_scalar(st[:, 31:32], st[:, 31:32], 127.0, OP.mult))
            g6 = V([g3], lambda e: e.tensor_single_scalar(st[:, 30:31], st[:, 30:31], 1.0 / 127.0, OP.mult))
            seq, t0 = c // CPS, (c % CPS) * CH
            for h in range(2):
                dep_q_free = rc.get("e3_0") if h == 1 else (R[c - 1].get("e3_1") if c >= 1 else None)
                e2 = S([g5, e1[2 * h + 1], dep_q_free], lambda e, h=h: e.activation(
                    qi_t[:], t1_t[:, 1024 * h:1024 * (h + 1)], AF.Identity, bias=0.0, scale=st[:, 31:32]))
                rc[f"e2_{h}"] = e2
                if h == 1:
                    rc["e2b"] = e2
                dep_r_free = rc.get("e4_0") if h == 1 else (R[c - 1].get("e4_1") if c >= 1 else None)
                e3 = S([e2, dep_r_free], lambda e: e.activation(r_t[:], qi_t[:], AF.Relu, bias=0.0, scale=1.0))
                rc[f"e3_{h}"] = e3
                dep_g_free = rc.get("sg0") if h == 1 else (R[c - 1].get("sg1") if c >= 1 else None)
                e4 = S([e3, g6, dep_g_free], lambda e: e.activation(
                    g_t[:], r_t[:], AF.Silu, bias=0.0, scale=st[:, 30:31]))
                rc[f"e4_{h}"] = e4
                if h == 1:
                    rc["e4b"] = e4
                sg = G([e4, rc["s1"], rc.get("tap_s")], lambda e, h=h: e.tensor_tensor(
                    sb_[:, 1024 * h:1024 * (h + 1)], g_t[:], sb_[:, 1024 * h:1024 * (h + 1)], OP.mult))
                rc[f"sg{h}"] = sg
                dep_o_free = R[c - 1].get(f"store{h}") if c >= 1 else None
                oo = G([sg, rc["xld"], dep_o_free], lambda e, h=h: e.tensor_tensor(
                    o_t[h][:], sb_[:, 1024 * h:1024 * (h + 1)],
                    xb[:, 1024 * h:1024 * (h + 1)], OP.add))
                rc[f"o{h}"] = oo
                rc[f"store{h}"] = dma(f"d_o{h}", [oo], dout[seq, t0:t0 + CH, 1024 * h:1024 * (h + 1)], o_t[h][:])
            rc["store"] = rc["store1"]

        if TRUNC == 1:
            for c in range(NCH):
                seq_, t0_ = c // CPS, (c % CPS) * CH
                for h in range(2):
                    dep = R[c - 1].get(f"store{h}") if c >= 1 else KQ_READY
                    mz = V([dep], lambda e, h=h: e.memset(o_t[h][:], 0.0))
                    R[c][f"store{h}"] = dma(f"d_o{h}", [mz], dout[seq_, t0_:t0_ + CH, 1024 * h:1024 * (h + 1)], o_t[h][:])
            R[NCH - 1]["store"] = R[NCH - 1]["store1"]
        if TRUNC == 0:
            emit_pro(0)
            for c in range(NCH):
                emit_mid(c)
                emit_big_half(c, 0)
                emit_mid2(c)
                emit_big_evacs(c, 0)
                emit_slast(c)
                emit_big_half(c, 1)
                if c + 1 < NCH:
                    emit_pro(c + 1)
                emit_big_evacs(c, 1)
                if DEBUG_TAPS and c == 0:
                    dsdbg = nc.dram_tensor("dbg_s", [128, D], F32, kind="ExternalOutput").ap()
                    R[0]["tap_s"] = dma("d_t", [R[0]["s1"]], dsdbg[:], s_t[0][:])
                emit_echunk(c)
        elif TRUNC != 1:
            for c in range(NCH):
                if c >= 1:
                    emit_echunk(c - 1)
                emit_pro(c)
                if TRUNC != 21:
                    emit_mid(c)
                    if "last" not in R[c]:
                        emit_mid2(c)
                        if "last" not in R[c]:
                            emit_slast(c)
                if TRUNC == 3:
                    emit_big_half(c, 0)
                    emit_big_evacs(c, 0)
                    emit_big_half(c, 1)
                    emit_big_evacs(c, 1)
            emit_echunk(NCH - 1)
        if DEBUG_TAPS:
            taps = [("dbg_st", st_t[0][:], F32), ("dbg_xn", xn_t[0][:], F32),
                    ("dbg_u8", u8_t[:], U8), ("dbg_ixqT", ixqT[0][:], BF16),
                    ("dbg_kq", kq[:], BF16), ("dbg_bqrow", bqrow[:], BF16),
                    ("dbg_usb", u_sb[:], F32), ("dbg_h", h_t[0][:], F32),
                    ("dbg_uA", uA_sb[:], F32), ("dbg_uB", uB_sb[:], F32),
                    ("dbg_hfv", hfv[:], F32), ("dbg_x2", x2_t[:], F32),
                    ("dbg_t1", t1_t[:], F32), ("dbg_qi", qi_t[:], I32),
                    ("dbg_g", g_t[:], F32)]
            last = R[NCH - 1]["store"]
            for nm, ap, dt_ in taps:
                dap = nc.dram_tensor(nm, list(ap.shape), dt_, kind="ExternalOutput").ap()
                last = dma("d_t", [last], dap, ap)

        P.finalize()
        with nc.Block() as block:
            @block.sync
            def _(sync):
                P.emit_engine("sync", sync, sems)
                sync.wait_ge(sems["d_o0"], 16 * NCH)
                sync.wait_ge(sems["d_o1"], 16 * NCH)

            @block.tensor
            def _(tensor):
                P.emit_engine("tensor", tensor, sems)

            @block.vector
            def _(vector):
                P.emit_engine("vector", vector, sems)

            @block.scalar
            def _(scalar):
                P.emit_engine("scalar", scalar, sems)

            @block.gpsimd
            def _(gpsimd):
                P.emit_engine("gpsimd", gpsimd, sems)

    return nc


_CACHE = {}


def _consts():
    eye_f = np.eye(128, dtype=np.float32)
    eye_b = np.eye(128, dtype=ml_dtypes.bfloat16)
    return eye_f, eye_b


def make_in_maps(inputs, n_cores=N_CORES):
    x = np.asarray(inputs["L0_attn_x"])
    B = x.shape[0]
    n_seq = B // n_cores
    eye_f, eye_b = _consts()

    def chunk_form(v):
        return np.ascontiguousarray(np.asarray(v, dtype=np.float32).reshape(ND, 128).T)

    base = {k: np.ascontiguousarray(np.asarray(v)) for k, v in inputs.items() if k != "L0_attn_x"}
    base["c_eye_f"] = eye_f
    base["c_eye_b"] = eye_b
    base["c_gamma_ch"] = chunk_form(inputs["L0_attn_ln_gamma"])
    base["c_beta_ch"] = chunk_form(inputs["L0_attn_ln_beta"])
    base["c_dskip_ch"] = chunk_form(inputs["L0_attn_D_skip"])
    base["c_bout_ch"] = chunk_form(inputs["L0_attn_b_out"])
    in_maps = []
    for i in range(n_cores):
        m = dict(base)
        m["L0_attn_x"] = np.ascontiguousarray(x[i * n_seq:(i + 1) * n_seq])
        in_maps.append(m)
    return in_maps


def kernel(**inputs):
    x = np.asarray(inputs["L0_attn_x"])
    B, S, _ = x.shape
    n_seq = B // N_CORES
    key = (n_seq, S)
    if key not in _CACHE:
        _CACHE[key] = build_nc(n_seq, S)
    nc = _CACHE[key]
    in_maps = make_in_maps(inputs)
    res = run_bass_kernel_spmd(nc, in_maps, core_ids=list(range(N_CORES)))
    return np.concatenate([r["out"] for r in res.results], axis=0)


# version 15
# speedup vs baseline: 1.1170x; 1.1170x over previous
import numpy as np
import ml_dtypes
import concourse.bass as bass
import concourse.mybir as mybir
from concourse.bass_utils import run_bass_kernel_spmd
from contextlib import ExitStack

F32 = mybir.dt.float32
BF16 = mybir.dt.bfloat16
U8 = mybir.dt.uint8
I32 = mybir.dt.int32
OP = mybir.AluOpType
AF = mybir.ActivationFunctionType
AX = mybir.AxisListType.X

D = 2048
ND = 16
CH = 128
EPS = 1e-5
MAGIC = 1597463007.0
N_CORES = 8
DEBUG_TAPS = False
TRUNC = 0
NEWTON_ITERS = 2


class Prog:
    ENGS = ("sync", "tensor", "vector", "scalar", "gpsimd")
    BARRIER_DMA = ("d_w", "d_w2")

    def __init__(self):
        self.ops = []

    def op(self, eng, deps, emit, inc=True, dma_sem=None):
        oid = len(self.ops)
        self.ops.append(dict(eng=eng, deps=[d for d in deps if d is not None],
                             emit=emit, inc=inc, dma=dma_sem))
        return oid

    def finalize(self):
        cnt = {e: 0 for e in self.ENGS}
        dma_cnt = {}
        for o in self.ops:
            if o["dma"] is not None:
                dma_cnt[o["dma"]] = dma_cnt.get(o["dma"], 0) + 1
                o["val"] = dma_cnt[o["dma"]] * 16
            elif o["inc"]:
                cnt[o["eng"]] += 1
                o["val"] = cnt[o["eng"]]
            else:
                o["val"] = None
        for i, o in enumerate(self.ops):
            waits = {}
            for d in o["deps"]:
                p = self.ops[d]
                assert d < i, f"dep {d} !< {i}"
                if (p["eng"] == o["eng"] == "tensor" and p["dma"] is None):
                    continue
                key = p["dma"] if p["dma"] is not None else p["eng"]
                assert p["val"] is not None, f"op {d} awaited but inc=False"
                waits[key] = max(waits.get(key, 0), p["val"])
            o["waits"] = waits
        for o in self.ops:
            for k in list(o["waits"]):
                if k in self.BARRIER_DMA:
                    o["waits"][k] = dma_cnt[k] * 16

    def emit_engine(self, ename, eng, sems):
        for o in self.ops:
            if o["eng"] != ename:
                continue
            for k, v in o["waits"].items():
                eng.wait_ge(sems[k], v)
            inst = o["emit"](eng)
            if o["dma"] is not None:
                inst.then_inc(sems[o["dma"]], 16)
            elif o["inc"]:
                inst.then_inc(sems[ename], 1)


def build_nc(n_seq, seq_len):
    TOK = n_seq * seq_len
    NCH = TOK // CH
    CPS = seq_len // CH
    NQ = 4

    nc = bass.Bass("TRN2", target_bir_lowering=False)

    dx = nc.dram_tensor("L0_attn_x", [n_seq, seq_len, D], F32, kind="ExternalInput").ap()
    dgam = nc.dram_tensor("L0_attn_ln_gamma", [D], F32, kind="ExternalInput").ap()
    dbet = nc.dram_tensor("L0_attn_ln_beta", [D], F32, kind="ExternalInput").ap()
    dwin = nc.dram_tensor("L0_attn_W_in", [D, 8], F32, kind="ExternalInput").ap()
    dbin = nc.dram_tensor("L0_attn_b_in", [8], F32, kind="ExternalInput").ap()
    dlog = nc.dram_tensor("L0_attn_decay_logit", [8], F32, kind="ExternalInput").ap()
    dwout = nc.dram_tensor("L0_attn_W_out", [8, D], F32, kind="ExternalInput").ap()
    dbout = nc.dram_tensor("L0_attn_b_out", [D], F32, kind="ExternalInput").ap()
    dskp = nc.dram_tensor("L0_attn_D_skip", [D], F32, kind="ExternalInput").ap()
    dwdn = nc.dram_tensor("L0_attn_W_down", [D, 32], F32, kind="ExternalInput").ap()
    dbdn = nc.dram_tensor("L0_attn_b_down", [32], F32, kind="ExternalInput").ap()
    dwup = nc.dram_tensor("L0_attn_W_up", [32, D], F32, kind="ExternalInput").ap()
    dbup = nc.dram_tensor("L0_attn_b_up", [D], F32, kind="ExternalInput").ap()
    dker = nc.dram_tensor("L0_ffn_kernel", [D, D], F32, kind="ExternalInput").ap()
    dbia = nc.dram_tensor("L0_ffn_bias", [D], F32, kind="ExternalInput").ap()
    deyef = nc.dram_tensor("c_eye_f", [128, 128], F32, kind="ExternalInput").ap()
    deyeb = nc.dram_tensor("c_eye_b", [128, 128], BF16, kind="ExternalInput").ap()
    dgamc = nc.dram_tensor("c_gamma_ch", [128, ND], F32, kind="ExternalInput").ap()
    dbetc = nc.dram_tensor("c_beta_ch", [128, ND], F32, kind="ExternalInput").ap()
    dskpc = nc.dram_tensor("c_dskip_ch", [128, ND], F32, kind="ExternalInput").ap()
    dboutc = nc.dram_tensor("c_bout_ch", [128, ND], F32, kind="ExternalInput").ap()
    dout = nc.dram_tensor("out", [n_seq, seq_len, D], F32, kind="ExternalOutput").ap()

    P = Prog()
    es = ExitStack()

    def sb(name, shape, dt):
        return es.enter_context(nc.sbuf_tensor(name, shape, dt))

    def pst(name, shape, dt):
        return es.enter_context(nc.psum_tensor(name, shape, dt))

    with es:
        kq = sb("kq", [128, ND * D], BF16)
        wcat = sb("wcat", [128, ND * 96], BF16)
        woutaug = sb("woutaug", [9, D], BF16)
        wupaug = sb("wupaug", [33, D], BF16)
        wowdg = sb("wowdg", [8, 32], BF16)
        outer_rhs = sb("outer_rhs", [2, D], BF16)
        bqrow = sb("bqrow", [1, D], BF16)
        c1row_bf = sb("c1row_bf", [1, D], BF16)
        swA96 = sb("swA96", [1, 96], BF16)
        eyef = sb("eyef", [128, 128], F32)
        eyeb = sb("eyeb", [128, 128], BF16)
        onesrow_b = sb("onesrow_b", [1, 128], BF16)
        ones32row = sb("ones32row", [1, 32], BF16)
        onescol_b = sb("onescol_b", [128, 1], BF16)
        csrow = sb("csrow", [1, 32], BF16)
        c8 = sb("c8", [8, 1], F32)
        cB = sb("cB", [32, 1], F32)
        cS_col = sb("cS_col", [32, 1], F32)
        decay = sb("decay", [8, 1], F32)
        decay_rep = sb("decay_rep", [8, CH], F32)
        gamc = sb("gamc", [128, ND], F32)
        betc = sb("betc", [128, ND], F32)
        skpc = sb("skpc", [128, ND], F32)
        boutc = sb("boutc", [128, ND], F32)
        woutT = sb("woutT", [128, ND * 8], BF16)
        gdc = sb("gdc", [128, ND], F32)
        bdc = sb("bdc", [128, ND], F32)
        c1c_bf = sb("c1c_bf", [128, ND], BF16)
        logit = sb("logit", [8, 1], F32)
        bin_t = sb("bin_t", [8, 1], F32)
        bdn_t = sb("bdn_t", [32, 1], F32)
        scal1 = sb("scal1", [8, 4], F32)
        cq128 = sb("cq128", [128, 1], F32)
        cm128 = sb("cm128", [128, 1], F32)
        bsc = sb("bsc", [1, 8], F32)
        kstat = sb("kstat", [128, 8], F32)
        wint = sb("wint", [128, ND * 8], F32)
        wdnt = sb("wdnt", [128, ND * 32], F32)
        x_t = [sb(f"x{i}", [128, D], F32) for i in range(2)]
        xn_t = [sb(f"xn{i}", [128, D], F32) for i in range(2)]
        xnb_t = sb("xnb", [128, D], BF16)
        u8_t = sb("u8", [128, D], U8)
        qbf_t = sb("qbf", [128, D], BF16)
        ixqT = [sb(f"ixqT{i}", [128, D], BF16) for i in range(2)]
        augrow = [sb(f"augrow{i}", [1, 128], BF16) for i in range(2)]
        xnT = sb("xnT", [128, D], BF16)
        s_t = [sb(f"s{i}", [128, D], F32) for i in range(2)]
        x2_t = sb("x2", [128, D], F32)
        x2tmp = sb("x2tmp", [128, 512], F32)
        sqscr = sb("sqscr", [128, 512], F32)
        t1_t = sb("t1", [128, D], F32)
        qi_t = sb("qi", [128, 1024], I32)
        r_t = sb("r", [128, 1024], BF16)
        g_t = sb("g", [128, 1024], F32)
        o_t = [sb(f"o{i}", [128, 1024], F32) for i in range(2)]
        t24 = [sb(f"t24_{i}", [128, 24], F32) for i in range(2)]
        st_t = [sb(f"st{i}", [128, 32], F32) for i in range(2)]
        nw_t = [sb(f"nw{i}", [128, 8], F32) for i in range(2)]
        nwi_t = [sb(f"nwi{i}", [128, 4], I32) for i in range(2)]
        u_sb = sb("u_sb", [32, CH], F32)
        uA_sb = sb("uA_sb", [32, CH], F32)
        uB_sb = sb("uB_sb", [32, CH], F32)
        hfv = sb("hfv", [32, CH], F32)
        h_t = [sb(f"h{i}", [8, CH], F32) for i in range(2)]
        haug = sb("haug", [9, CH], BF16)
        hfaug = sb("hfaug", [33, CH], BF16)
        r2row = sb("r2row", [1, 128], BF16)
        rm2row = sb("rm2row", [1, 128], BF16)
        r1aug = sb("r1aug", [2, 128], BF16)
        row_beta = xn_t[0][0:1, :]
        row_bout = x2_t[0:1, :]
        row_dskip = t1_t[0:1, :]
        row_gamma = s_t[0][0:1, :]
        wout_f = x_t[0][0:8, :]
        wup_f = x_t[1][0:32, :]
        kcol = [xn_t[1], s_t[1]]
        bqu8 = u8_t[0:1, :]
        ku8 = u8_t
        kT = [qbf_t, ixqT[0]]

        py = [pst(f"py{i}", [128, 512], F32) for i in range(2)]
        pb2 = pst("pb2", [128, 512], F32)
        pb3 = pst("pb3", [128, 512], F32)
        pb4 = pst("pb4", [128, 512], F32)
        pff = [pst(f"pff{i}", [128, 512], F32) for i in range(3)]
        prep1 = pff[0][0:32, 0:128]
        prep2 = pff[0][0:32, 128:256]
        prow = pff[1][0:1, 0:128]
        pT2 = pff[2][0:32, 0:128]
        pw8 = py[0][0:8, 0:32]
        pw32 = py[0][0:32, 32:33]
        pw1 = py[1][0:1, 0:32]
        pw128 = py[1][0:128, 8:16]
        pbwd = pb4

        sems = {}
        for e in Prog.ENGS:
            sems[e] = es.enter_context(nc.semaphore(f"sem_{e}"))
        for dname in ("d_w", "d_w2", "d_x0", "d_x1", "d_o0", "d_o1", "d_k0", "d_k1", "d_t"):
            sems[dname] = es.enter_context(nc.semaphore(dname))

        def dma(sem, deps, out, in_):
            return P.op("sync", deps, lambda e, o=out, i=in_: e.dma_start(out=o, in_=i),
                        dma_sem=sem)

        def T(deps, fn, inc=False):
            return P.op("tensor", deps, fn, inc=inc)

        def V(deps, fn, inc=True):
            return P.op("vector", deps, fn, inc=inc)

        def S(deps, fn, inc=True):
            return P.op("scalar", deps, fn, inc=inc)

        def G(deps, fn, inc=True):
            return P.op("gpsimd", deps, fn, inc=inc)

        L = {}
        L["eyef"] = dma("d_w", [], eyef[:], deyef[:])
        L["eyeb"] = dma("d_w", [], eyeb[:], deyeb[:])
        L["gamc"] = dma("d_w", [], gamc[:], dgamc[:])
        L["betc"] = dma("d_w", [], betc[:], dbetc[:])
        L["skpc"] = dma("d_w", [], skpc[:], dskpc[:])
        L["boutc"] = dma("d_w", [], boutc[:], dboutc[:])
        for j in range(ND):
            L[f"win{j}"] = dma("d_w", [], wint[:, 8 * j:8 * j + 8], dwin[128 * j:128 * (j + 1), :])
            L[f"wdn{j}"] = dma("d_w", [], wdnt[:, 32 * j:32 * j + 32], dwdn[128 * j:128 * (j + 1), :])
        L["wout"] = dma("d_w", [], wout_f, dwout[:])
        L["wup"] = dma("d_w", [], wup_f, dwup[:])
        L["brow"] = dma("d_w", [], row_beta, dbet.rearrange("(a b) -> a b", a=1))
        L["borow"] = dma("d_w", [], row_bout, dbout.rearrange("(a b) -> a b", a=1))
        L["dsrow"] = dma("d_w", [], row_dskip, dskp.rearrange("(a b) -> a b", a=1))
        L["garow"] = dma("d_w", [], row_gamma, dgam.rearrange("(a b) -> a b", a=1))
        L["logit"] = dma("d_w", [], logit[:], dlog.rearrange("(a b) -> a b", b=1))
        L["bin"] = dma("d_w", [], bin_t[:], dbin.rearrange("(a b) -> a b", b=1))
        L["bdn"] = dma("d_w", [], bdn_t[:], dbdn.rearrange("(a b) -> a b", b=1))

        c_ones1 = V([], lambda e: e.memset(onesrow_b[:], 1.0))
        c_o32 = V([], lambda e: e.memset(ones32row[:], 1.0))
        c_oc = V([], lambda e: e.memset(onescol_b[:], 1.0))
        c_r1a = V([], lambda e: e.memset(r1aug[:], 1.0))
        c_haug = V([], lambda e: e.memset(haug[:], 1.0))
        c_hfau = V([], lambda e: e.memset(hfaug[32:33, :], 1.0))
        c_q128 = V([], lambda e: e.memset(cq128[:], 128.0))
        c_sw0 = V([], lambda e: e.memset(swA96[:], 0.0))
        c_m128 = V([], lambda e: e.memset(cm128[:], -128.0))
        c_wz = V([], lambda e: e.memset(wcat[:], 0.0))

        p_e = S([L["logit"]], lambda e: e.activation(scal1[:, 0:1], logit[:], AF.Exp, bias=0.0, scale=-1.0))
        p_e2 = V([p_e], lambda e: e.tensor_single_scalar(scal1[:, 1:2], scal1[:, 0:1], 1.0, OP.add))
        p_dec = V([p_e2], lambda e: e.reciprocal(decay[:], scal1[:, 1:2]))
        p_dr1 = V([], lambda e: e.memset(decay_rep[:], 1.0))
        p_dr2 = V([p_dr1, p_dec], lambda e: e.tensor_scalar(decay_rep[:], decay_rep[:], decay[:], None, OP.mult))

        p_gdr = V([L["garow"], L["dsrow"]], lambda e: e.tensor_tensor(row_gamma, row_gamma, row_dskip, OP.mult))
        p_bdr = V([L["brow"], L["dsrow"]], lambda e: e.tensor_tensor(row_beta, row_beta, row_dskip, OP.mult))
        p_c1r = V([p_bdr, L["borow"]], lambda e: e.tensor_tensor(row_bout, row_bout, row_beta, OP.add))
        p_orhs1 = V([], lambda e: e.memset(outer_rhs[:], 1.0))
        p_orhs0 = V([p_gdr, p_orhs1], lambda e: e.tensor_copy(outer_rhs[0:1, :], row_gamma))
        L["burow"] = dma("d_w2", [p_gdr, p_bdr], row_dskip, dbup.rearrange("(a b) -> a b", a=1))
        L["birow"] = dma("d_w2", [p_c1r], row_beta, dbia.rearrange("(a b) -> a b", a=1))
        p_wo = V([L["wout"]], lambda e: e.tensor_copy(woutaug[0:8, :], wout_f))
        p_wo1v = V([p_c1r], lambda e: e.tensor_copy(c1row_bf[:], row_bout))
        p_wo1 = dma("d_w2", [p_wo1v], woutaug[8:9, :], c1row_bf[:])
        p_wu = V([L["wup"]], lambda e: e.tensor_copy(wupaug[0:32, :], wup_f))
        p_wu1 = V([L["burow"]], lambda e: e.tensor_copy(wupaug[32:33, :], row_dskip))
        p_gdc = V([L["gamc"], L["skpc"]], lambda e: e.tensor_tensor(gdc[:], gamc[:], skpc[:], OP.mult))
        p_bdc = V([L["betc"], L["skpc"]], lambda e: e.tensor_tensor(bdc[:], betc[:], skpc[:], OP.mult))
        p_c1c = V([p_bdc, L["boutc"]], lambda e: e.tensor_tensor(bdc[:], boutc[:], bdc[:], OP.add))
        p_c1cb = V([p_c1c], lambda e: e.tensor_copy(c1c_bf[:], bdc[:]))
        p_wcat = []
        for j in range(ND):
            p_wcat.append(V([L[f"win{j}"], L["gamc"], c_wz],
                            lambda e, j=j: e.tensor_scalar(wcat[:, 96 * j:96 * j + 8], wint[:, 8 * j:8 * j + 8], gamc[:, j:j + 1], None, OP.mult)))
            p_wcat.append(V([L[f"wdn{j}"], L["gamc"], c_wz],
                            lambda e, j=j: e.tensor_scalar(wcat[:, 96 * j + 32:96 * j + 64], wdnt[:, 32 * j:32 * j + 32], gamc[:, j:j + 1], None, OP.mult)))
            p_wcat.append(V([L[f"wdn{j}"], p_gdc, c_wz],
                            lambda e, j=j: e.tensor_scalar(wcat[:, 96 * j + 64:96 * j + 96], wdnt[:, 32 * j:32 * j + 32], gdc[:, j:j + 1], None, OP.mult)))

        mm = None
        for j in range(ND):
            mm = T([L[f"win{j}"], L["betc"], mm],
                   lambda e, j=j: e.matmul(pw8[0:8, 0:1], wint[:, 8 * j:8 * j + 8], betc[:, j:j + 1],
                                           start=(j == 0), stop=(j == ND - 1), skip_group_check=True), inc=(j == ND - 1))
        p_c8 = V([mm, L["bin"]], lambda e: e.tensor_tensor(c8[:], pw8[0:8, 0:1], bin_t[:], OP.add))
        mm = None
        for j in range(ND):
            mm = T([L[f"wdn{j}"], L["betc"], p_c8 if j == 0 else mm],
                   lambda e, j=j: e.matmul(pw32[:], wdnt[:, 32 * j:32 * j + 32], betc[:, j:j + 1],
                                           start=(j == 0), stop=(j == ND - 1), skip_group_check=True), inc=(j == ND - 1))
        p_cB = V([mm, L["bdn"]], lambda e: e.tensor_tensor(cB[:], pw32[:], bdn_t[:], OP.add))
        mm = None
        for j in range(ND):
            mm = T([p_wcat[3 * j + 1], c_oc, p_cB if j == 0 else mm],
                   lambda e, j=j: e.matmul(pw32[:], wcat[:, 96 * j + 32:96 * j + 64], onescol_b[:],
                                           start=(j == 0), stop=(j == ND - 1), skip_group_check=True), inc=(j == ND - 1))
        p_cSv = V([mm], lambda e: e.tensor_copy(cS_col[:], pw32[:]))
        p_cSr = T([p_cSv, L["eyef"]], lambda e: e.transpose(pw1[:], cS_col[:], eyef[0:32, 0:32]), inc=True)
        p_cSe = V([p_cSr], lambda e: e.tensor_copy(csrow[:], pw1[:]))
        p_wT = []
        prev = p_cSe
        for j in range(ND):
            t = T([L["wout"], L["eyef"], prev], lambda e, j=j: e.transpose(pw128[:], wout_f[:, 128 * j:128 * (j + 1)], eyef[0:8, 0:8]), inc=True)
            prev = V([t], lambda e, j=j: e.tensor_copy(woutT[:, 8 * j:8 * j + 8], pw128[:]))
            p_wT.append(prev)
        mm = None
        for j in range(ND):
            mm = T([p_wT[j], p_wcat[3 * j + 1], p_cSr if j == 0 else mm],
                   lambda e, j=j: e.matmul(pw8[:], woutT[:, 8 * j:8 * j + 8], wcat[:, 96 * j + 32:96 * j + 64],
                                           start=(j == 0), stop=(j == ND - 1), skip_group_check=True), inc=(j == ND - 1))
        p_wow = V([mm], lambda e: e.tensor_copy(wowdg[:], pw8[:]))
        mm = None
        for j in range(ND):
            mm = T([p_c1cb, p_wcat[3 * j + 1], p_wT[-1] if j == 0 else mm],
                   lambda e, j=j: e.matmul(pw1[:], c1c_bf[:, j:j + 1], wcat[:, 96 * j + 32:96 * j + 64],
                                           start=(j == 0), stop=(j == ND - 1), skip_group_check=True), inc=(j == ND - 1))
        p_cw2 = V([mm, c_sw0], lambda e: e.tensor_copy(swA96[0:1, 32:64], pw1[:]))
        p_bam = V([L["birow"]], lambda e: e.reduce_max(bsc[:, 0:1], row_beta, axis=AX, apply_absolute_value=True))
        p_bg = V([p_bam], lambda e: e.tensor_single_scalar(bsc[:, 1:2], bsc[:, 0:1], 1e-30, OP.max))
        p_bre = V([p_bg], lambda e: e.reciprocal(bsc[:, 2:3], bsc[:, 1:2]))
        p_binv = V([p_bre], lambda e: e.tensor_single_scalar(bsc[:, 3:4], bsc[:, 2:3], 127.0, OP.mult))
        p_bsc = V([p_bg], lambda e: e.tensor_single_scalar(bsc[:, 4:5], bsc[:, 1:2], 1.0 / 127.0, OP.mult))
        p_bnb = V([p_bsc], lambda e: e.tensor_single_scalar(bsc[:, 5:6], bsc[:, 4:5], -128.0, OP.mult))
        p_bq1 = S([p_binv, L["birow"], c_q128], lambda e: e.activation(bqu8[:], row_beta, AF.Identity, bias=cq128[0:1, :], scale=bsc[:, 3:4]))
        p_bq2 = S([p_bq1, p_bnb], lambda e: e.activation(bqrow[:], bqu8[:], AF.Identity, bias=bsc[:, 5:6], scale=bsc[:, 4:5]))

        kqr = kq[:].rearrange("p (dj f) -> p dj f", dj=ND)
        hist = []
        for fi in range(ND):
            h = {}
            buf = kcol[fi % 2]
            prev_q0 = hist[fi - 1]["q2"] if fi >= 1 else p_bq2
            prev_k6 = hist[fi - 1]["k6"] if fi >= 1 else None
            loads = []
            for djg in range(ND):
                dep_prev = hist[fi - 2]["bwd_last"] if fi >= 2 else None
                dep_first = p_wcat[-1] if (fi <= 1 and djg == 0) else None
                loads.append(dma(f"d_k{fi % 2}", [dep_prev, dep_first],
                                 buf[:, 128 * djg:128 * (djg + 1)],
                                 dker[128 * djg:128 * (djg + 1), 128 * fi:128 * (fi + 1)]))
            egrp = []
            tlast = None
            for g in range(4):
                tt = None
                for b in range(4):
                    dj = 4 * g + b
                    deps = [loads[-1], L["eyef"]]
                    if b == 0:
                        if g == 0:
                            deps.append(hist[fi - 1]["fwd_evac"][3] if fi >= 1 else None)
                        else:
                            deps.append(egrp[g - 1])
                        if fi >= 2:
                            deps.append(hist[fi - 2]["q1"])
                    tt = T(deps, lambda e, dj=dj, b=b, buf=buf: e.transpose(
                        pb2[:, 128 * b:128 * (b + 1)], buf[:, 128 * dj:128 * (dj + 1)], eyef[:]),
                        inc=(b == 3))
                egrp.append(V([tt, prev_k6 if g == 0 else None], lambda e, g=g, fi=fi: e.tensor_copy(
                    kT[fi % 2][:, 512 * g:512 * (g + 1)], pb2[:])))
                tlast = tt
            h["fwd_evac"] = egrp
            h["fwd_last"] = tlast
            k1 = V([egrp[0], egrp[1], egrp[2], egrp[3], prev_q0, prev_k6],
                   lambda e, fi=fi: e.reduce_max(kstat[:, 4:5], kT[fi % 2][:], axis=AX, apply_absolute_value=True))
            k2 = V([k1], lambda e: e.tensor_single_scalar(kstat[:, 4:5], kstat[:, 4:5], 1e-30, OP.max))
            k3 = V([k2], lambda e: e.reciprocal(kstat[:, 5:6], kstat[:, 4:5]))
            k4 = V([k3], lambda e: e.tensor_single_scalar(kstat[:, 5:6], kstat[:, 5:6], 127.0, OP.mult))
            k5 = V([k2], lambda e: e.tensor_single_scalar(kstat[:, 6:7], kstat[:, 4:5], 1.0 / 127.0, OP.mult))
            k6 = V([k5], lambda e: e.tensor_single_scalar(kstat[:, 7:8], kstat[:, 6:7], -128.0, OP.mult))
            h["k6"] = k6
            q1 = S([k4, egrp[-1], prev_q0, c_q128],
                   lambda e, fi=fi: e.activation(ku8[:], kT[fi % 2][:], AF.Identity, bias=cq128[:], scale=kstat[:, 5:6]))
            h["q1"] = q1
            kqf = buf
            q2 = S([q1, k6, tlast], lambda e, kqf=kqf: e.activation(kqf[:], ku8[:], AF.Identity, bias=kstat[:, 7:8], scale=kstat[:, 6:7]))
            h["q2"] = q2
            bgrp = []
            blast = None
            for g in range(4):
                tt = None
                for b in range(4):
                    dj = 4 * g + b
                    deps = [q2, L["eyef"]]
                    if b == 0:
                        if g == 0:
                            deps.append(hist[fi - 1]["bwd_evac"][3] if fi >= 1 else None)
                        else:
                            deps.append(bgrp[g - 1])
                    tt = T(deps, lambda e, dj=dj, b=b, kqf=kqf: e.transpose(
                        pbwd[:, 128 * b:128 * (b + 1)], kqf[:, 128 * dj:128 * (dj + 1)], eyef[:]),
                        inc=(b == 3))
                ev = V([tt], lambda e, g=g, fi=fi: e.tensor_copy(
                    kqr[:, 4 * g:4 * g + 4, 128 * fi:128 * (fi + 1)],
                    pbwd[:].rearrange("p (a b) -> p a b", a=4)))
                bgrp.append(ev)
                blast = tt
            h["bwd_evac"] = bgrp
            h["bwd_last"] = blast
            hist.append(h)
        KQ_READY = hist[-1]["bwd_evac"][-1]
        KT0_FREE = hist[14]["q2"]
        KT1_FREE = hist[15]["q2"]
        KCOL0_FREE = hist[14]["bwd_last"]
        KCOL1_FREE = hist[15]["bwd_last"]

        R = [dict() for _ in range(NCH)]

        def emit_pro(c):
            rc = R[c]
            seq, t0 = c // CPS, (c % CPS) * CH
            xb = x_t[c % 2]
            xn = xn_t[c % 2]
            xdeps = []
            if c >= 2:
                xdeps += [R[c - 2].get("o1"), R[c - 2].get("s1"), R[c - 2].get("e2b")]
            if c == 0:
                xdeps += [p_wo, p_wT[-1]]
            if c == 1:
                xdeps += [p_wu]
            rc["xld"] = dma(f"d_x{c % 2}", xdeps, xb[:], dx[seq, t0:t0 + CH, :])
            st = st_t[c % 2]
            tb = t24[c % 2]
            nw = nw_t[c % 2]
            nwi = nwi_t[c % 2]
            dep_st = R[c - 2].get("e4b") if c >= 2 else None
            dep_st2 = R[c - 2].get("e3_1") if c >= 2 else None
            bn = []
            for q in range(4):
                bn.append(V([rc["xld"], dep_st, dep_st2], lambda e, q=q: e.bn_stats(tb[:, 6 * q:6 * q + 6], xb[:, 512 * q:512 * (q + 1)])))
            ag = V([bn[0], bn[1], bn[2], bn[3]], lambda e: e.bn_aggr(st[:, 0:2], tb[:]))
            c1_ = V([ag], lambda e: e.tensor_single_scalar(nw[:, 0:1], st[:, 1:2], EPS, OP.add))
            c2_ = V([c1_], lambda e: e.reciprocal(st[:, 5:6], nw[:, 0:1]))
            c3_ = V([c2_], lambda e: e.tensor_tensor(st[:, 6:7], st[:, 1:2], st[:, 5:6], OP.mult))
            c4_ = V([c3_], lambda e: e.tensor_single_scalar(nw[:, 1:2], st[:, 6:7], EPS, OP.add))
            rcp = V([c4_], lambda e: e.reciprocal(nw[:, 4:6], nw[:, 0:2]))
            nwd = S([rcp], lambda e: e.activation(st[:, 7:9], nw[:, 4:6], AF.Sqrt, bias=0.0, scale=1.0))
            rc["st0"] = nwd
            x1 = V([nwd], lambda e: e.tensor_tensor(st[:, 9:10], st[:, 0:1], st[:, 7:8], OP.mult))
            x2b_ = V([x1], lambda e: e.tensor_single_scalar(st[:, 9:10], st[:, 9:10], -1.0, OP.mult))
            dep_xn1 = R[c - 2].get("x2a3") if c >= 2 else None
            dep_xn2 = R[c - 2].get("q8") if c >= 2 else None
            dep_xn3 = R[c - 2].get("xnbd") if c >= 2 else None
            if c == 0:
                dep_xn1 = p_bq2
            if c == 1:
                dep_xn2 = KCOL0_FREE
            rc["xn"] = S([x2b_, rc["xld"], dep_xn1, dep_xn2, dep_xn3],
                         lambda e, xn=xn: e.activation(xn[:], xb[:], AF.Identity, bias=st[:, 9:10], scale=st[:, 7:8]))
            if TRUNC == 21:
                rc["last"] = [rc["xn"]]
                return
            dep_xnb = R[c - 1].get("t1b") if c >= 1 else None
            rc["xnbd"] = S([rc["xn"], dep_xnb], lambda e, xn=xn: e.activation(xnb_t[:], xn[:], AF.Identity, bias=0.0, scale=1.0))
            rc["am"] = V([rc["xn"]], lambda e, xn=xn: e.reduce_max(st[:, 10:11], xn[:], axis=AX, apply_absolute_value=True))
            v1 = V([rc["am"]], lambda e: e.tensor_single_scalar(st[:, 10:11], st[:, 10:11], 1e-30, OP.max))
            v2 = V([v1], lambda e: e.reciprocal(st[:, 11:12], st[:, 10:11]))
            v3 = V([v2], lambda e: e.tensor_single_scalar(st[:, 11:12], st[:, 11:12], 127.0, OP.mult))
            v4 = V([v1], lambda e: e.tensor_single_scalar(st[:, 12:13], st[:, 10:11], 1.0 / 127.0, OP.mult))
            rc["inv0"] = v3
            rc["sc0"] = v4
            dep_u8 = R[c - 1].get("qbf") if c >= 1 else p_bq2
            rc["q8"] = S([v3, dep_u8, c_q128], lambda e, xn=xn: e.activation(u8_t[:], xn[:], AF.Identity, bias=cq128[:], scale=st[:, 11:12]))
            dep_qbf = R[c - 1].get("t2b") if c >= 1 else KT0_FREE
            rc["qbf"] = S([rc["q8"], dep_qbf, c_m128], lambda e: e.activation(qbf_t[:], u8_t[:], AF.Identity, bias=cm128[:], scale=1.0))
            return

        def emit_mid(c):
            rc = R[c]
            st = st_t[c % 2]
            nw = nw_t[c % 2]
            nwi = nwi_t[c % 2]
            xn = xn_t[c % 2]
            nwd = rc["st0"]
            tb2 = [pb2, pb4]
            dep_pT0 = R[c - 1].get("t2e2") if c >= 1 else KQ_READY
            dep_pT1 = [R[c - 1].get("t2e3"), R[c - 1].get("s1")] if c >= 1 else [KQ_READY]
            dep_xnT_free = R[c - 1].get("swB") if c >= 1 else None
            t1e = []
            tlast = None
            for g in range(4):
                bank = tb2[g % 2]
                for b in range(4):
                    j = 4 * g + b
                    deps = [rc["xnbd"], L["eyeb"]]
                    if b == 0:
                        if g == 0:
                            deps.append(dep_pT0)
                        elif g == 1:
                            deps += dep_pT1
                        else:
                            deps.append(t1e[g - 2])
                    tlast = T(deps, lambda e, j=j, b=b, bank=bank: e.matmul(
                        bank[:, 128 * b:128 * (b + 1)], xnb_t[:, 128 * j:128 * (j + 1)], eyeb[:],
                        start=True, stop=True, skip_group_check=True), inc=(b == 3))
                t1e.append(V([tlast, dep_xnT_free],
                             lambda e, g=g, bank=bank: e.tensor_copy(xnT[:, 512 * g:512 * (g + 1)], bank[:])))
            rc["t1b"] = tlast
            ixb = ixqT[c % 2]
            dep_ix_free = R[c - 2].get("bigb") if c >= 2 else (KT1_FREE if c == 0 else None)
            t2e = []
            tlast = None
            for g in range(4):
                bank = tb2[g % 2]
                for b in range(4):
                    j = 4 * g + b
                    deps = [rc["qbf"], L["eyeb"]]
                    if b == 0:
                        deps.append(t1e[g + 2] if g < 2 else t2e[g - 2])
                    tlast = T(deps, lambda e, j=j, b=b, bank=bank: e.matmul(
                        bank[:, 128 * b:128 * (b + 1)], qbf_t[:, 128 * j:128 * (j + 1)], eyeb[:],
                        start=True, stop=True, skip_group_check=True), inc=(b == 3))
                t2e.append(V([tlast, dep_ix_free],
                             lambda e, g=g, ixb=ixb, bank=bank: e.tensor_copy(ixb[:, 512 * g:512 * (g + 1)], bank[:])))
            rc["t2b"] = tlast
            rc["t2e2"] = t2e[2]
            rc["t2e3"] = t2e[3]
            rc["t2e"] = t2e
            rc["ixb"] = ixb
            if TRUNC == 22:
                rc["last"] = [t2e[3], t1e[3]]
                return
            dep_prow = R[c - 1].get("rm2e") if c >= 1 else None
            dep_prow2 = R[c - 1].get("s1") if c >= 1 else None
            rc["augT"] = T([rc["inv0"], dep_prow, dep_prow2, L["eyef"]],
                           lambda e: e.transpose(prow[:], st[:, 11:12], eyef[:]), inc=True)
            dep_aug_free = R[c - 2].get("bigb") if c >= 2 else None
            rc["auge"] = S([rc["augT"], dep_aug_free],
                           lambda e: e.activation(augrow[c % 2][:], prow[:], AF.Identity, bias=0.0, scale=1.0))
            rc["r1T"] = T([nwd, rc["auge"], L["eyef"]],
                          lambda e: e.transpose(prow[:], st[:, 8:9], eyef[:]), inc=True)
            rc["r1a"] = V([rc["r1T"], c_r1a], lambda e: e.tensor_copy(r1aug[0:1, :], prow[:]))
            if TRUNC == 221:
                rc["last"] = [rc["r1a"], rc["auge"]]
                return
            dep_prep = R[c - 1].get("hf1") if c >= 1 else None
            rc["rep1"] = T([rc["r1a"], c_o32, dep_prep],
                           lambda e: e.matmul(prep1[:], ones32row[:], r1aug[0:1, :], start=True, stop=True, skip_group_check=True), inc=True)
            if TRUNC == 222:
                rc["rep1e"] = V([rc["rep1"]], lambda e: e.tensor_copy(u_sb[0:8, :], prep1[0:8, :]))
                rc["last"] = [rc["rep1e"]]
                return
            dep_u_free = R[c - 1].get("uBe") if c >= 1 else None
            swu = None
            for j in range(ND):
                deps = [t1e[j // 4], p_wcat[3 * j], p_wcat[3 * j + 1], p_wcat[3 * j + 2]]
                if j == 0:
                    deps.append(dep_u_free)
                swu = T(deps, lambda e, j=j: e.matmul(
                    pb3[0:96, 0:128], wcat[:, 96 * j:96 * (j + 1)], xnT[:, 128 * j:128 * (j + 1)],
                    start=(j == 0), stop=False, skip_group_check=True), inc=False)
            swu = T([swu, c_ones1, p_cw2], lambda e: e.matmul(
                pb3[0:96, 0:128], swA96[:], onesrow_b[:], start=False, stop=False, skip_group_check=True), inc=True)
            rc["swB"] = swu
            rc["ue"] = V([swu], lambda e: e.tensor_copy(u_sb[:], pb3[0:32, 0:128]))
            um = V([rc["ue"], rc["rep1"]], lambda e: e.tensor_tensor(u_sb[0:8, :], u_sb[0:8, :], prep1[0:8, :], OP.mult))
            uc = V([um, p_c8], lambda e: e.tensor_scalar(u_sb[0:8, :], u_sb[0:8, :], c8[:], None, OP.add))
            hb_ = h_t[c % 2]
            init = 0.0 if c % CPS == 0 else h_t[(c - 1) % 2][:, CH - 1:CH]
            dep_h = R[c - 2].get("scan") if c >= 2 else None
            rc["scan"] = V([uc, p_dr2, dep_h], lambda e, hb_=hb_: e.tensor_tensor_scan(hb_[:], decay_rep[:], u_sb[0:8, :], init, OP.mult, OP.add))
            dep_ha = R[c - 1].get("xarlast") if c >= 1 else None
            dep_ha2 = R[c - 1].get("wow") if c >= 1 else None
            rc["hb"] = V([rc["scan"], dep_ha, dep_ha2, c_haug], lambda e, hb_=hb_: e.tensor_copy(haug[0:8, :], hb_[:]))
            rc["wow"] = T([rc["hb"], p_wow, swu], lambda e: e.matmul(
                pb3[32:64, 0:128], wowdg[:], haug[0:8, :], start=False, stop=True, skip_group_check=True), inc=True)
            rc["uAe"] = V([rc["wow"]], lambda e: e.tensor_copy(uA_sb[:], pb3[32:64, 0:128]))
            rc["uBe"] = V([rc["uAe"]], lambda e: e.tensor_copy(uB_sb[:], pb3[64:96, 0:128]))
            if TRUNC == 23:
                rc["last"] = [rc["uBe"], rc["auge"]]
                return
            e1prev = R[c - 1].get("e1list") if c >= 1 else None
            prev_a = prev_b = None
            for q in range(NQ):
                deps_o = [rc["r1a"], p_orhs0, t2e[3]]
                if q == 0:
                    deps_o.append(e1prev[2] if e1prev else (KQ_READY if c == 0 else None))
                else:
                    deps_o.append(prev_a)
                mo = T(deps_o, lambda e, q=q: e.matmul(py[0][:], r1aug[:], outer_rhs[:, 512 * q:512 * (q + 1)], start=True, stop=True, skip_group_check=True), inc=True)
                deps_x = [rc["hb"], p_wo, p_wo1]
                if q == 0:
                    deps_x.append(e1prev[3] if e1prev else None)
                else:
                    deps_x.append(prev_b)
                mx = T(deps_x, lambda e, q=q: e.matmul(py[1][:], haug[:], woutaug[:, 512 * q:512 * (q + 1)], start=True, stop=True, skip_group_check=True), inc=True)
                va = V([mo, rc["xn"], prev_b], lambda e, q=q, xn=xn: e.tensor_tensor(x2tmp[:], xn[:, 512 * q:512 * (q + 1)], py[0][:], OP.mult))
                vb = V([va, mx], lambda e, q=q: e.scalar_tensor_tensor(
                    x2_t[:, 512 * q:512 * (q + 1)], x2tmp[:], 1.0, py[1][:], OP.mult, OP.add,
                    accum_out=st[:, 13 + q:14 + q]))
                prev_a, prev_b = va, vb
                rc[f"x2a{q}"] = va
                rc[f"x2b{q}"] = vb
                rc["xarlast"] = mx
            return

        def emit_mid2(c):
            rc = R[c]
            st = st_t[c % 2]
            nw = nw_t[c % 2]
            nwi = nwi_t[c % 2]
            if TRUNC == 24:
                rc["last"] = [rc["x2b3"], rc["x2a3"]]
                return
            sq = []
            for q in range(4):
                deps = [rc[f"x2b{q}"]]
                if q == 0:
                    deps.append(R[c - 1].get("e3_1") if c >= 1 else KCOL1_FREE)
                else:
                    deps.append(sq[q - 1])
                sq.append(S(deps, lambda e, q=q: e.activation(sqscr[:], x2_t[:, 512 * q:512 * (q + 1)], AF.Square,
                                                              bias=0.0, scale=1.0, accum_out=st[:, 17 + q:18 + q])))
            rc["sq1"] = sq[3]
            s1_ = V([rc["x2b3"]], lambda e: e.reduce_sum(st[:, 21:22], st[:, 13:17], axis=AX))
            s2_ = V([s1_], lambda e: e.tensor_single_scalar(st[:, 21:22], st[:, 21:22], 1.0 / D, OP.mult))
            s3_ = V([sq[3]], lambda e: e.reduce_sum(st[:, 22:23], st[:, 17:21], axis=AX))
            s4_ = V([s3_], lambda e: e.tensor_single_scalar(st[:, 22:23], st[:, 22:23], 1.0 / D, OP.mult))
            s5_ = V([s2_], lambda e: e.tensor_tensor(st[:, 23:24], st[:, 21:22], st[:, 21:22], OP.mult))
            s6_ = V([s4_, s5_], lambda e: e.tensor_tensor(st[:, 22:23], st[:, 22:23], st[:, 23:24], OP.subtract))
            s7_ = V([s6_], lambda e: e.tensor_single_scalar(nw[:, 6:7], st[:, 22:23], EPS, OP.add))
            rcp2 = V([s7_], lambda e: e.reciprocal(nw[:, 7:8], nw[:, 6:7]))
            s8_ = S([rcp2], lambda e: e.activation(st[:, 24:25], nw[:, 7:8], AF.Sqrt, bias=0.0, scale=1.0))
            s9_ = V([s8_], lambda e: e.tensor_tensor(st[:, 25:26], st[:, 24:25], st[:, 21:22], OP.mult))
            rT1 = T([s8_, rc["r1T"], L["eyef"]], lambda e: e.transpose(prow[:], st[:, 24:25], eyef[:]), inc=True)
            rc["r2e"] = V([rT1], lambda e: e.tensor_copy(r2row[:], prow[:]))
            rT2 = T([s9_, rc["r2e"], L["eyef"]], lambda e: e.transpose(prow[:], st[:, 25:26], eyef[:]), inc=True)
            rc["rm2e"] = V([rT2], lambda e: e.tensor_copy(rm2row[:], prow[:]))
            dep_pr2 = R[c - 1].get("hf1") if c >= 1 else None
            rep2 = T([rc["r2e"], c_o32, dep_pr2], lambda e: e.matmul(prep2[:], ones32row[:], r2row[:], start=True, stop=True, skip_group_check=True), inc=True)
            dep_pT2 = R[c - 1].get("hf1") if c >= 1 else None
            repT2 = T([rc["rm2e"], p_cSe, dep_pT2], lambda e: e.matmul(pT2[:], csrow[:], rm2row[:], start=True, stop=True, skip_group_check=True), inc=True)
            f1 = V([rc["uBe"], rc["rep1"], rep2], lambda e: e.tensor_tensor(hfv[:], uB_sb[:], prep1[:], OP.mult))
            f2 = V([f1, rc["uAe"]], lambda e: e.tensor_tensor(hfv[:], hfv[:], uA_sb[:], OP.add))
            f3 = V([f2, rep2], lambda e: e.tensor_tensor(hfv[:], hfv[:], prep2[:], OP.mult))
            f4 = V([f3, repT2, p_cB], lambda e: e.scalar_tensor_tensor(hfv[:], hfv[:], cB[:], pT2[:], OP.add, OP.subtract))
            dep_hfa = R[c - 1].get("wup3") if c >= 1 else None
            f5 = V([f4, dep_hfa, c_hfau], lambda e: e.tensor_scalar(hfaug[0:32, :], hfv[:], 0.0, None, OP.max))
            rc["hf1"] = f5
            if TRUNC == 25:
                rc["last"] = [f5]
                return
            prev_s = R[c - 1].get("s1") if c >= 1 else None
            wdst = [pb4, pff[0], pff[1], pff[2]]
            for q in range(NQ):
                deps = [f5, p_wu, p_wu1, prev_s]
                if q == 0:
                    deps += [rc["x2b3"], rc["t2e3"]]
                    if c == 0:
                        deps.append(KQ_READY)
                wu = T(deps, lambda e, q=q: e.matmul(
                    wdst[q][:], hfaug[:], wupaug[:, 512 * q:512 * (q + 1)], start=True, stop=True, skip_group_check=True), inc=True)
                rc[f"wup{q}"] = wu
            sb_ = s_t[c % 2]
            dep_s_free = R[c - 2].get("sg1") if c >= 2 else (KCOL1_FREE if c == 1 else None)
            slast = None
            for q in range(NQ):
                slast = V([rc[f"wup{q}"], dep_s_free, slast], lambda e, q=q: e.tensor_tensor(
                    sb_[:, 512 * q:512 * (q + 1)], x2_t[:, 512 * q:512 * (q + 1)], wdst[q][:], OP.add))
            rc["s1"] = slast
            return

        def emit_big_half(c, hf_):
            rc = R[c]
            st = st_t[c % 2]
            t2e = rc["t2e"]
            ixb = rc["ixb"]
            dep_y0 = R[c - 1].get("e1b") if c >= 1 else KQ_READY
            dep_t1free = R[c - 1].get("e2b") if c >= 1 else None
            e1 = rc.setdefault("e1list", [])
            mmlast = None
            for j in range(ND):
                for qq in range(2):
                    fofs = 1024 * hf_ + 512 * qq
                    deps = [t2e[j // 4], KQ_READY]
                    if j == 0 and qq == 0:
                        deps.append(dep_y0 if hf_ == 0 else e1[-1])
                        deps += [rc["x2a3"], rc["x2b3"]]
                        if c == 0:
                            deps += [p_c8, p_cw2, p_cSe, p_wow]
                    mmlast = T(deps, lambda e, j=j, qq=qq, fofs=fofs, ixb=ixb: e.matmul(
                        py[qq][:], ixb[:, 128 * j:128 * (j + 1)],
                        kq[:, D * j + fofs:D * j + fofs + 512],
                        start=(j == 0), stop=False, skip_group_check=True), inc=False)
            for qq in range(2):
                fofs = 1024 * hf_ + 512 * qq
                mmlast = T([mmlast, rc["auge"], p_bq2], lambda e, qq=qq, fofs=fofs: e.matmul(
                    py[qq][:], augrow[c % 2][:], bqrow[:, fofs:fofs + 512],
                    start=False, stop=True, skip_group_check=True), inc=(qq == 1))
            rc[f"big{hf_}"] = mmlast

        def emit_big_evacs(c, hf_):
            rc = R[c]
            st = st_t[c % 2]
            dep_t1free = R[c - 1].get("e2b") if c >= 1 else None
            e1 = rc["e1list"]
            mmlast = rc[f"big{hf_}"]
            e1.append(V([mmlast, rc["sc0"], dep_t1free], lambda e, hf_=hf_: e.tensor_scalar(
                t1_t[:, 1024 * hf_:1024 * hf_ + 512], py[0][:], st[:, 12:13], None, OP.mult, OP.max,
                accum_out=st[:, 26 + 2 * hf_:27 + 2 * hf_])))
            e1.append(V([e1[-1]], lambda e, hf_=hf_: e.tensor_scalar(
                t1_t[:, 1024 * hf_ + 512:1024 * (hf_ + 1)], py[1][:], st[:, 12:13], None, OP.mult, OP.max,
                accum_out=st[:, 27 + 2 * hf_:28 + 2 * hf_])))
            if hf_ == 1:
                rc["bigb"] = rc["big1"]
                rc["e1b"] = e1[-1]

        def emit_echunk(c):
            rc = R[c]
            st = st_t[c % 2]
            xb = x_t[c % 2]
            sb_ = s_t[c % 2]
            seq_, t0_ = c // CPS, (c % CPS) * CH
            if TRUNC >= 21:
                for h in range(2):
                    dep_o_free = R[c - 1].get(f"store{h}") if c >= 1 else None
                    oo = V(rc["last"] + [dep_o_free], lambda e, h=h: e.tensor_copy(
                        o_t[h][:], xn_t[c % 2][:, 1024 * h:1024 * (h + 1)]))
                    rc[f"store{h}"] = dma(f"d_o{h}", [oo], dout[seq_, t0_:t0_ + CH, 1024 * h:1024 * (h + 1)], o_t[h][:])
                rc["store"] = rc["store1"]
                return
            if TRUNC == 2:
                for h in range(2):
                    dep_o_free = R[c - 1].get(f"store{h}") if c >= 1 else None
                    oo = V([rc["s1"], dep_o_free], lambda e, h=h: e.tensor_copy(
                        o_t[h][:], sb_[:, 1024 * h:1024 * (h + 1)]))
                    rc[f"sg{h}"] = oo
                    rc[f"o{h}"] = oo
                    rc[f"store{h}"] = dma(f"d_o{h}", [oo], dout[seq_, t0_:t0_ + CH, 1024 * h:1024 * (h + 1)], o_t[h][:])
                rc["store"] = rc["store1"]
                return
            if TRUNC == 3:
                e1_ = rc["e1list"]
                for h in range(2):
                    dep_o_free = R[c - 1].get(f"store{h}") if c >= 1 else None
                    oo = V([e1_[2 * h + 1], dep_o_free], lambda e, h=h: e.tensor_copy(
                        o_t[h][:], t1_t[:, 1024 * h:1024 * (h + 1)]))
                    rc[f"o{h}"] = oo
                    if h == 1:
                        rc["e2b"] = oo
                    rc[f"store{h}"] = dma(f"d_o{h}", [oo], dout[seq_, t0_:t0_ + CH, 1024 * h:1024 * (h + 1)], o_t[h][:])
                rc["store"] = rc["store1"]
                return
            e1 = rc["e1list"]
            g1 = V([e1[3]], lambda e: e.reduce_max(st[:, 30:31], st[:, 26:30], axis=AX))
            g2 = V([g1], lambda e: e.tensor_single_scalar(st[:, 30:31], st[:, 30:31], 0.0, OP.max))
            g3 = V([g2], lambda e: e.tensor_single_scalar(st[:, 30:31], st[:, 30:31], 1e-30, OP.max))
            g4 = V([g3], lambda e: e.reciprocal(st[:, 31:32], st[:, 30:31]))
            g5 = V([g4], lambda e: e.tensor_single_scalar(st[:, 31:32], st[:, 31:32], 127.0, OP.mult))
            g6 = V([g3], lambda e: e.tensor_single_scalar(st[:, 30:31], st[:, 30:31], 1.0 / 127.0, OP.mult))
            seq, t0 = c // CPS, (c % CPS) * CH
            for h in range(2):
                dep_q_free = rc.get("e3_0") if h == 1 else (R[c - 1].get("e3_1") if c >= 1 else None)
                e2 = S([g5, e1[2 * h + 1], dep_q_free], lambda e, h=h: e.activation(
                    qi_t[:], t1_t[:, 1024 * h:1024 * (h + 1)], AF.Identity, bias=0.0, scale=st[:, 31:32]))
                rc[f"e2_{h}"] = e2
                if h == 1:
                    rc["e2b"] = e2
                dep_r_free = rc.get("e4_0") if h == 1 else (R[c - 1].get("e4_1") if c >= 1 else None)
                e3 = S([e2, dep_r_free], lambda e: e.activation(r_t[:], qi_t[:], AF.Relu, bias=0.0, scale=1.0))
                rc[f"e3_{h}"] = e3
                dep_g_free = rc.get("sg0") if h == 1 else (R[c - 1].get("sg1") if c >= 1 else None)
                e4 = S([e3, g6, dep_g_free], lambda e: e.activation(
                    g_t[:], r_t[:], AF.Silu, bias=0.0, scale=st[:, 30:31]))
                rc[f"e4_{h}"] = e4
                if h == 1:
                    rc["e4b"] = e4
                sg = G([e4, rc["s1"], rc.get("tap_s")], lambda e, h=h: e.tensor_tensor(
                    sb_[:, 1024 * h:1024 * (h + 1)], g_t[:], sb_[:, 1024 * h:1024 * (h + 1)], OP.mult))
                rc[f"sg{h}"] = sg
                dep_o_free = R[c - 1].get(f"store{h}") if c >= 1 else None
                oo = G([sg, rc["xld"], dep_o_free], lambda e, h=h: e.tensor_tensor(
                    o_t[h][:], sb_[:, 1024 * h:1024 * (h + 1)],
                    xb[:, 1024 * h:1024 * (h + 1)], OP.add))
                rc[f"o{h}"] = oo
                rc[f"store{h}"] = dma(f"d_o{h}", [oo], dout[seq, t0:t0 + CH, 1024 * h:1024 * (h + 1)], o_t[h][:])
            rc["store"] = rc["store1"]

        if TRUNC == 1:
            for c in range(NCH):
                seq_, t0_ = c // CPS, (c % CPS) * CH
                for h in range(2):
                    dep = R[c - 1].get(f"store{h}") if c >= 1 else KQ_READY
                    mz = V([dep], lambda e, h=h: e.memset(o_t[h][:], 0.0))
                    R[c][f"store{h}"] = dma(f"d_o{h}", [mz], dout[seq_, t0_:t0_ + CH, 1024 * h:1024 * (h + 1)], o_t[h][:])
            R[NCH - 1]["store"] = R[NCH - 1]["store1"]
        if TRUNC == 0:
            emit_pro(0)
            for c in range(NCH):
                emit_mid(c)
                emit_big_half(c, 0)
                emit_mid2(c)
                emit_big_evacs(c, 0)
                emit_big_half(c, 1)
                if c + 1 < NCH:
                    emit_pro(c + 1)
                emit_big_evacs(c, 1)
                if DEBUG_TAPS and c == 0:
                    dsdbg = nc.dram_tensor("dbg_s", [128, D], F32, kind="ExternalOutput").ap()
                    R[0]["tap_s"] = dma("d_t", [R[0]["s1"]], dsdbg[:], s_t[0][:])
                emit_echunk(c)
        elif TRUNC != 1:
            for c in range(NCH):
                if c >= 1:
                    emit_echunk(c - 1)
                emit_pro(c)
                if TRUNC != 21:
                    emit_mid(c)
                    if "last" not in R[c]:
                        emit_mid2(c)
                if TRUNC == 3:
                    emit_big_half(c, 0)
                    emit_big_evacs(c, 0)
                    emit_big_half(c, 1)
                    emit_big_evacs(c, 1)
            emit_echunk(NCH - 1)
        if DEBUG_TAPS:
            taps = [("dbg_st", st_t[0][:], F32), ("dbg_xn", xn_t[0][:], F32),
                    ("dbg_u8", u8_t[:], U8), ("dbg_ixqT", ixqT[0][:], BF16),
                    ("dbg_kq", kq[:], BF16), ("dbg_bqrow", bqrow[:], BF16),
                    ("dbg_usb", u_sb[:], F32), ("dbg_h", h_t[0][:], F32),
                    ("dbg_uA", uA_sb[:], F32), ("dbg_uB", uB_sb[:], F32),
                    ("dbg_hfv", hfv[:], F32), ("dbg_x2", x2_t[:], F32),
                    ("dbg_t1", t1_t[:], F32), ("dbg_qi", qi_t[:], I32),
                    ("dbg_g", g_t[:], F32)]
            last = R[NCH - 1]["store"]
            for nm, ap, dt_ in taps:
                dap = nc.dram_tensor(nm, list(ap.shape), dt_, kind="ExternalOutput").ap()
                last = dma("d_t", [last], dap, ap)

        P.finalize()
        with nc.Block() as block:
            @block.sync
            def _(sync):
                P.emit_engine("sync", sync, sems)
                sync.wait_ge(sems["d_o0"], 16 * NCH)
                sync.wait_ge(sems["d_o1"], 16 * NCH)

            @block.tensor
            def _(tensor):
                P.emit_engine("tensor", tensor, sems)

            @block.vector
            def _(vector):
                P.emit_engine("vector", vector, sems)

            @block.scalar
            def _(scalar):
                P.emit_engine("scalar", scalar, sems)

            @block.gpsimd
            def _(gpsimd):
                P.emit_engine("gpsimd", gpsimd, sems)

    return nc


_CACHE = {}


def _consts():
    eye_f = np.eye(128, dtype=np.float32)
    eye_b = np.eye(128, dtype=ml_dtypes.bfloat16)
    return eye_f, eye_b


def make_in_maps(inputs, n_cores=N_CORES):
    x = np.asarray(inputs["L0_attn_x"])
    B = x.shape[0]
    n_seq = B // n_cores
    eye_f, eye_b = _consts()

    def chunk_form(v):
        return np.ascontiguousarray(np.asarray(v, dtype=np.float32).reshape(ND, 128).T)

    base = {k: np.ascontiguousarray(np.asarray(v)) for k, v in inputs.items() if k != "L0_attn_x"}
    base["c_eye_f"] = eye_f
    base["c_eye_b"] = eye_b
    base["c_gamma_ch"] = chunk_form(inputs["L0_attn_ln_gamma"])
    base["c_beta_ch"] = chunk_form(inputs["L0_attn_ln_beta"])
    base["c_dskip_ch"] = chunk_form(inputs["L0_attn_D_skip"])
    base["c_bout_ch"] = chunk_form(inputs["L0_attn_b_out"])
    in_maps = []
    for i in range(n_cores):
        m = dict(base)
        m["L0_attn_x"] = np.ascontiguousarray(x[i * n_seq:(i + 1) * n_seq])
        in_maps.append(m)
    return in_maps


def kernel(**inputs):
    x = np.asarray(inputs["L0_attn_x"])
    B, S, _ = x.shape
    n_seq = B // N_CORES
    key = (n_seq, S)
    if key not in _CACHE:
        _CACHE[key] = build_nc(n_seq, S)
    nc = _CACHE[key]
    in_maps = make_in_maps(inputs)
    res = run_bass_kernel_spmd(nc, in_maps, core_ids=list(range(N_CORES)))
    return np.concatenate([r["out"] for r in res.results], axis=0)


# version 16 (final)
# speedup vs baseline: 1.1283x; 1.0101x over previous
import numpy as np
import ml_dtypes
import concourse.bass as bass
import concourse.mybir as mybir
from concourse.bass_utils import run_bass_kernel_spmd
from contextlib import ExitStack

F32 = mybir.dt.float32
BF16 = mybir.dt.bfloat16
U8 = mybir.dt.uint8
I32 = mybir.dt.int32
OP = mybir.AluOpType
AF = mybir.ActivationFunctionType
AX = mybir.AxisListType.X

D = 2048
ND = 16
CH = 128
EPS = 1e-5
MAGIC = 1597463007.0
N_CORES = 8
DEBUG_TAPS = False
TRUNC = 0
NEWTON_ITERS = 2


class Prog:
    ENGS = ("sync", "tensor", "vector", "scalar", "gpsimd")
    BARRIER_DMA = ("d_w", "d_w2")

    def __init__(self):
        self.ops = []

    def op(self, eng, deps, emit, inc=True, dma_sem=None):
        oid = len(self.ops)
        self.ops.append(dict(eng=eng, deps=[d for d in deps if d is not None],
                             emit=emit, inc=inc, dma=dma_sem))
        return oid

    def finalize(self):
        cnt = {e: 0 for e in self.ENGS}
        dma_cnt = {}
        for o in self.ops:
            if o["dma"] is not None:
                dma_cnt[o["dma"]] = dma_cnt.get(o["dma"], 0) + 1
                o["val"] = dma_cnt[o["dma"]] * 16
            elif o["inc"]:
                cnt[o["eng"]] += 1
                o["val"] = cnt[o["eng"]]
            else:
                o["val"] = None
        for i, o in enumerate(self.ops):
            waits = {}
            for d in o["deps"]:
                p = self.ops[d]
                assert d < i, f"dep {d} !< {i}"
                if (p["eng"] == o["eng"] == "tensor" and p["dma"] is None):
                    continue
                key = p["dma"] if p["dma"] is not None else p["eng"]
                assert p["val"] is not None, f"op {d} awaited but inc=False"
                waits[key] = max(waits.get(key, 0), p["val"])
            o["waits"] = waits
        for o in self.ops:
            for k in list(o["waits"]):
                if k in self.BARRIER_DMA:
                    o["waits"][k] = dma_cnt[k] * 16

    def emit_engine(self, ename, eng, sems):
        for o in self.ops:
            if o["eng"] != ename:
                continue
            for k, v in o["waits"].items():
                eng.wait_ge(sems[k], v)
            inst = o["emit"](eng)
            if o["dma"] is not None:
                inst.then_inc(sems[o["dma"]], 16)
            elif o["inc"]:
                inst.then_inc(sems[ename], 1)


def build_nc(n_seq, seq_len):
    TOK = n_seq * seq_len
    NCH = TOK // CH
    CPS = seq_len // CH
    NQ = 4

    nc = bass.Bass("TRN2", target_bir_lowering=False)

    dx = nc.dram_tensor("L0_attn_x", [n_seq, seq_len, D], F32, kind="ExternalInput").ap()
    dgam = nc.dram_tensor("L0_attn_ln_gamma", [D], F32, kind="ExternalInput").ap()
    dbet = nc.dram_tensor("L0_attn_ln_beta", [D], F32, kind="ExternalInput").ap()
    dwin = nc.dram_tensor("L0_attn_W_in", [D, 8], F32, kind="ExternalInput").ap()
    dbin = nc.dram_tensor("L0_attn_b_in", [8], F32, kind="ExternalInput").ap()
    dlog = nc.dram_tensor("L0_attn_decay_logit", [8], F32, kind="ExternalInput").ap()
    dwout = nc.dram_tensor("L0_attn_W_out", [8, D], F32, kind="ExternalInput").ap()
    dbout = nc.dram_tensor("L0_attn_b_out", [D], F32, kind="ExternalInput").ap()
    dskp = nc.dram_tensor("L0_attn_D_skip", [D], F32, kind="ExternalInput").ap()
    dwdn = nc.dram_tensor("L0_attn_W_down", [D, 32], F32, kind="ExternalInput").ap()
    dbdn = nc.dram_tensor("L0_attn_b_down", [32], F32, kind="ExternalInput").ap()
    dwup = nc.dram_tensor("L0_attn_W_up", [32, D], F32, kind="ExternalInput").ap()
    dbup = nc.dram_tensor("L0_attn_b_up", [D], F32, kind="ExternalInput").ap()
    dker = nc.dram_tensor("L0_ffn_kernel", [D, D], F32, kind="ExternalInput").ap()
    dbia = nc.dram_tensor("L0_ffn_bias", [D], F32, kind="ExternalInput").ap()
    deyef = nc.dram_tensor("c_eye_f", [128, 128], F32, kind="ExternalInput").ap()
    deyeb = nc.dram_tensor("c_eye_b", [128, 128], BF16, kind="ExternalInput").ap()
    dgamc = nc.dram_tensor("c_gamma_ch", [128, ND], F32, kind="ExternalInput").ap()
    dbetc = nc.dram_tensor("c_beta_ch", [128, ND], F32, kind="ExternalInput").ap()
    dskpc = nc.dram_tensor("c_dskip_ch", [128, ND], F32, kind="ExternalInput").ap()
    dboutc = nc.dram_tensor("c_bout_ch", [128, ND], F32, kind="ExternalInput").ap()
    dout = nc.dram_tensor("out", [n_seq, seq_len, D], F32, kind="ExternalOutput").ap()

    P = Prog()
    es = ExitStack()

    def sb(name, shape, dt):
        return es.enter_context(nc.sbuf_tensor(name, shape, dt))

    def pst(name, shape, dt):
        return es.enter_context(nc.psum_tensor(name, shape, dt))

    with es:
        kq = sb("kq", [128, ND * D], BF16)
        wcat = sb("wcat", [128, ND * 96], BF16)
        woutaug = sb("woutaug", [9, D], BF16)
        wupaug = sb("wupaug", [33, D], BF16)
        wowdg = sb("wowdg", [8, 32], BF16)
        outer_rhs = sb("outer_rhs", [2, D], BF16)
        bqrow = sb("bqrow", [1, D], BF16)
        c1row_bf = sb("c1row_bf", [1, D], BF16)
        swA96 = sb("swA96", [1, 96], BF16)
        eyef = sb("eyef", [128, 128], F32)
        eyeb = sb("eyeb", [128, 128], BF16)
        onesrow_b = sb("onesrow_b", [1, 128], BF16)
        ones32row = sb("ones32row", [1, 32], BF16)
        onescol_b = sb("onescol_b", [128, 1], BF16)
        csrow = sb("csrow", [1, 32], BF16)
        c8 = sb("c8", [8, 1], F32)
        cB = sb("cB", [32, 1], F32)
        cS_col = sb("cS_col", [32, 1], F32)
        decay = sb("decay", [8, 1], F32)
        decay_rep = sb("decay_rep", [8, CH], F32)
        gamc = sb("gamc", [128, ND], F32)
        betc = sb("betc", [128, ND], F32)
        skpc = sb("skpc", [128, ND], F32)
        boutc = sb("boutc", [128, ND], F32)
        woutT = sb("woutT", [128, ND * 8], BF16)
        gdc = sb("gdc", [128, ND], F32)
        bdc = sb("bdc", [128, ND], F32)
        c1c_bf = sb("c1c_bf", [128, ND], BF16)
        logit = sb("logit", [8, 1], F32)
        bin_t = sb("bin_t", [8, 1], F32)
        bdn_t = sb("bdn_t", [32, 1], F32)
        scal1 = sb("scal1", [8, 4], F32)
        cq128 = sb("cq128", [128, 1], F32)
        cm128 = sb("cm128", [128, 1], F32)
        bsc = sb("bsc", [1, 8], F32)
        kstat = sb("kstat", [128, 8], F32)
        wint = sb("wint", [128, ND * 8], F32)
        wdnt = sb("wdnt", [128, ND * 32], F32)
        x_t = [sb(f"x{i}", [128, D], F32) for i in range(2)]
        xn_t = [sb(f"xn{i}", [128, D], F32) for i in range(2)]
        xnb_t = sb("xnb", [128, D], BF16)
        u8_t = sb("u8", [128, D], U8)
        qbf_t = sb("qbf", [128, D], BF16)
        ixqT = [sb(f"ixqT{i}", [128, D], BF16) for i in range(2)]
        augrow = [sb(f"augrow{i}", [1, 128], BF16) for i in range(2)]
        xnT = sb("xnT", [128, D], BF16)
        s_t = [sb(f"s{i}", [128, D], F32) for i in range(2)]
        x2_t = sb("x2", [128, D], F32)
        x2tmp = sb("x2tmp", [128, 512], F32)
        sqscr = sb("sqscr", [128, 512], F32)
        t1_t = sb("t1", [128, D], F32)
        qi_t = sb("qi", [128, 1024], I32)
        r_t = sb("r", [128, 1024], BF16)
        g_t = sb("g", [128, 1024], F32)
        o_t = [sb(f"o{i}", [128, 1024], F32) for i in range(2)]
        t24 = [sb(f"t24_{i}", [128, 24], F32) for i in range(2)]
        st_t = [sb(f"st{i}", [128, 32], F32) for i in range(2)]
        nw_t = [sb(f"nw{i}", [128, 8], F32) for i in range(2)]
        nwi_t = [sb(f"nwi{i}", [128, 4], I32) for i in range(2)]
        u_sb = sb("u_sb", [32, CH], F32)
        uA_sb = sb("uA_sb", [32, CH], F32)
        uB_sb = sb("uB_sb", [32, CH], F32)
        hfv = sb("hfv", [32, CH], F32)
        h_t = [sb(f"h{i}", [8, CH], F32) for i in range(2)]
        haug = sb("haug", [9, CH], BF16)
        hfaug = sb("hfaug", [33, CH], BF16)
        r2row = sb("r2row", [1, 128], BF16)
        rm2row = sb("rm2row", [1, 128], BF16)
        r1aug = sb("r1aug", [2, 128], BF16)
        row_beta = xn_t[0][0:1, :]
        row_bout = x2_t[0:1, :]
        row_dskip = t1_t[0:1, :]
        row_gamma = s_t[0][0:1, :]
        wout_f = x_t[0][0:8, :]
        wup_f = x_t[1][0:32, :]
        kcol = [xn_t[1], s_t[1]]
        bqu8 = u8_t[0:1, :]
        ku8 = u8_t
        kT = [qbf_t, ixqT[0]]

        py = [pst(f"py{i}", [128, 512], F32) for i in range(2)]
        pb2 = pst("pb2", [128, 512], F32)
        pb3 = pst("pb3", [128, 512], F32)
        pb4 = pst("pb4", [128, 512], F32)
        pff = [pst(f"pff{i}", [128, 512], F32) for i in range(3)]
        prep1 = pff[0][0:32, 0:128]
        prep2 = pff[0][0:32, 128:256]
        prow = pff[1][0:1, 0:128]
        pT2 = pff[2][0:32, 0:128]
        pw8 = py[0][0:8, 0:32]
        pw32 = py[0][0:32, 32:33]
        pw1 = py[1][0:1, 0:32]
        pw128 = py[1][0:128, 8:16]
        pbwd = pb4

        sems = {}
        for e in Prog.ENGS:
            sems[e] = es.enter_context(nc.semaphore(f"sem_{e}"))
        for dname in ("d_w", "d_w2", "d_x0", "d_x1", "d_o0", "d_o1", "d_k0", "d_k1", "d_t"):
            sems[dname] = es.enter_context(nc.semaphore(dname))

        def dma(sem, deps, out, in_):
            return P.op("sync", deps, lambda e, o=out, i=in_: e.dma_start(out=o, in_=i),
                        dma_sem=sem)

        def T(deps, fn, inc=False):
            return P.op("tensor", deps, fn, inc=inc)

        def V(deps, fn, inc=True):
            return P.op("vector", deps, fn, inc=inc)

        def S(deps, fn, inc=True):
            return P.op("scalar", deps, fn, inc=inc)

        def G(deps, fn, inc=True):
            return P.op("gpsimd", deps, fn, inc=inc)

        L = {}
        L["eyef"] = dma("d_w", [], eyef[:], deyef[:])
        L["eyeb"] = dma("d_w", [], eyeb[:], deyeb[:])
        L["gamc"] = dma("d_w", [], gamc[:], dgamc[:])
        L["betc"] = dma("d_w", [], betc[:], dbetc[:])
        L["skpc"] = dma("d_w", [], skpc[:], dskpc[:])
        L["boutc"] = dma("d_w", [], boutc[:], dboutc[:])
        for j in range(ND):
            L[f"win{j}"] = dma("d_w", [], wint[:, 8 * j:8 * j + 8], dwin[128 * j:128 * (j + 1), :])
            L[f"wdn{j}"] = dma("d_w", [], wdnt[:, 32 * j:32 * j + 32], dwdn[128 * j:128 * (j + 1), :])
        L["wout"] = dma("d_w", [], wout_f, dwout[:])
        L["wup"] = dma("d_w", [], wup_f, dwup[:])
        L["brow"] = dma("d_w", [], row_beta, dbet.rearrange("(a b) -> a b", a=1))
        L["borow"] = dma("d_w", [], row_bout, dbout.rearrange("(a b) -> a b", a=1))
        L["dsrow"] = dma("d_w", [], row_dskip, dskp.rearrange("(a b) -> a b", a=1))
        L["garow"] = dma("d_w", [], row_gamma, dgam.rearrange("(a b) -> a b", a=1))
        L["logit"] = dma("d_w", [], logit[:], dlog.rearrange("(a b) -> a b", b=1))
        L["bin"] = dma("d_w", [], bin_t[:], dbin.rearrange("(a b) -> a b", b=1))
        L["bdn"] = dma("d_w", [], bdn_t[:], dbdn.rearrange("(a b) -> a b", b=1))

        c_ones1 = V([], lambda e: e.memset(onesrow_b[:], 1.0))
        c_o32 = V([], lambda e: e.memset(ones32row[:], 1.0))
        c_oc = V([], lambda e: e.memset(onescol_b[:], 1.0))
        c_r1a = V([], lambda e: e.memset(r1aug[:], 1.0))
        c_haug = V([], lambda e: e.memset(haug[:], 1.0))
        c_hfau = V([], lambda e: e.memset(hfaug[32:33, :], 1.0))
        c_q128 = V([], lambda e: e.memset(cq128[:], 128.0))
        c_sw0 = V([], lambda e: e.memset(swA96[:], 0.0))
        c_m128 = V([], lambda e: e.memset(cm128[:], -128.0))
        c_wz = V([], lambda e: e.memset(wcat[:], 0.0))

        p_e = S([L["logit"]], lambda e: e.activation(scal1[:, 0:1], logit[:], AF.Exp, bias=0.0, scale=-1.0))
        p_e2 = V([p_e], lambda e: e.tensor_single_scalar(scal1[:, 1:2], scal1[:, 0:1], 1.0, OP.add))
        p_dec = V([p_e2], lambda e: e.reciprocal(decay[:], scal1[:, 1:2]))
        p_dr1 = V([], lambda e: e.memset(decay_rep[:], 1.0))
        p_dr2 = V([p_dr1, p_dec], lambda e: e.tensor_scalar(decay_rep[:], decay_rep[:], decay[:], None, OP.mult))

        p_gdr = V([L["garow"], L["dsrow"]], lambda e: e.tensor_tensor(row_gamma, row_gamma, row_dskip, OP.mult))
        p_bdr = V([L["brow"], L["dsrow"]], lambda e: e.tensor_tensor(row_beta, row_beta, row_dskip, OP.mult))
        p_c1r = V([p_bdr, L["borow"]], lambda e: e.tensor_tensor(row_bout, row_bout, row_beta, OP.add))
        p_orhs1 = V([], lambda e: e.memset(outer_rhs[:], 1.0))
        p_orhs0 = V([p_gdr, p_orhs1], lambda e: e.tensor_copy(outer_rhs[0:1, :], row_gamma))
        L["burow"] = dma("d_w2", [p_gdr, p_bdr], row_dskip, dbup.rearrange("(a b) -> a b", a=1))
        L["birow"] = dma("d_w2", [p_c1r], row_beta, dbia.rearrange("(a b) -> a b", a=1))
        p_wo = V([L["wout"]], lambda e: e.tensor_copy(woutaug[0:8, :], wout_f))
        p_wo1v = V([p_c1r], lambda e: e.tensor_copy(c1row_bf[:], row_bout))
        p_wo1 = dma("d_w2", [p_wo1v], woutaug[8:9, :], c1row_bf[:])
        p_wu = V([L["wup"]], lambda e: e.tensor_copy(wupaug[0:32, :], wup_f))
        p_wu1 = V([L["burow"]], lambda e: e.tensor_copy(wupaug[32:33, :], row_dskip))
        p_gdc = V([L["gamc"], L["skpc"]], lambda e: e.tensor_tensor(gdc[:], gamc[:], skpc[:], OP.mult))
        p_bdc = V([L["betc"], L["skpc"]], lambda e: e.tensor_tensor(bdc[:], betc[:], skpc[:], OP.mult))
        p_c1c = V([p_bdc, L["boutc"]], lambda e: e.tensor_tensor(bdc[:], boutc[:], bdc[:], OP.add))
        p_c1cb = V([p_c1c], lambda e: e.tensor_copy(c1c_bf[:], bdc[:]))
        p_wcat = []
        for j in range(ND):
            p_wcat.append(V([L[f"win{j}"], L["gamc"], c_wz],
                            lambda e, j=j: e.tensor_scalar(wcat[:, 96 * j:96 * j + 8], wint[:, 8 * j:8 * j + 8], gamc[:, j:j + 1], None, OP.mult)))
            p_wcat.append(V([L[f"wdn{j}"], L["gamc"], c_wz],
                            lambda e, j=j: e.tensor_scalar(wcat[:, 96 * j + 32:96 * j + 64], wdnt[:, 32 * j:32 * j + 32], gamc[:, j:j + 1], None, OP.mult)))
            p_wcat.append(V([L[f"wdn{j}"], p_gdc, c_wz],
                            lambda e, j=j: e.tensor_scalar(wcat[:, 96 * j + 64:96 * j + 96], wdnt[:, 32 * j:32 * j + 32], gdc[:, j:j + 1], None, OP.mult)))

        mm = None
        for j in range(ND):
            mm = T([L[f"win{j}"], L["betc"], mm],
                   lambda e, j=j: e.matmul(pw8[0:8, 0:1], wint[:, 8 * j:8 * j + 8], betc[:, j:j + 1],
                                           start=(j == 0), stop=(j == ND - 1), skip_group_check=True), inc=(j == ND - 1))
        p_c8 = V([mm, L["bin"]], lambda e: e.tensor_tensor(c8[:], pw8[0:8, 0:1], bin_t[:], OP.add))
        mm = None
        for j in range(ND):
            mm = T([L[f"wdn{j}"], L["betc"], p_c8 if j == 0 else mm],
                   lambda e, j=j: e.matmul(pw32[:], wdnt[:, 32 * j:32 * j + 32], betc[:, j:j + 1],
                                           start=(j == 0), stop=(j == ND - 1), skip_group_check=True), inc=(j == ND - 1))
        p_cB = V([mm, L["bdn"]], lambda e: e.tensor_tensor(cB[:], pw32[:], bdn_t[:], OP.add))
        mm = None
        for j in range(ND):
            mm = T([p_wcat[3 * j + 1], c_oc, p_cB if j == 0 else mm],
                   lambda e, j=j: e.matmul(pw32[:], wcat[:, 96 * j + 32:96 * j + 64], onescol_b[:],
                                           start=(j == 0), stop=(j == ND - 1), skip_group_check=True), inc=(j == ND - 1))
        p_cSv = V([mm], lambda e: e.tensor_copy(cS_col[:], pw32[:]))
        p_cSr = T([p_cSv, L["eyef"]], lambda e: e.transpose(pw1[:], cS_col[:], eyef[0:32, 0:32]), inc=True)
        p_cSe = V([p_cSr], lambda e: e.tensor_copy(csrow[:], pw1[:]))
        p_wT = []
        prev = p_cSe
        for j in range(ND):
            t = T([L["wout"], L["eyef"], prev], lambda e, j=j: e.transpose(pw128[:], wout_f[:, 128 * j:128 * (j + 1)], eyef[0:8, 0:8]), inc=True)
            prev = V([t], lambda e, j=j: e.tensor_copy(woutT[:, 8 * j:8 * j + 8], pw128[:]))
            p_wT.append(prev)
        mm = None
        for j in range(ND):
            mm = T([p_wT[j], p_wcat[3 * j + 1], p_cSr if j == 0 else mm],
                   lambda e, j=j: e.matmul(pw8[:], woutT[:, 8 * j:8 * j + 8], wcat[:, 96 * j + 32:96 * j + 64],
                                           start=(j == 0), stop=(j == ND - 1), skip_group_check=True), inc=(j == ND - 1))
        p_wow = V([mm], lambda e: e.tensor_copy(wowdg[:], pw8[:]))
        mm = None
        for j in range(ND):
            mm = T([p_c1cb, p_wcat[3 * j + 1], p_wT[-1] if j == 0 else mm],
                   lambda e, j=j: e.matmul(pw1[:], c1c_bf[:, j:j + 1], wcat[:, 96 * j + 32:96 * j + 64],
                                           start=(j == 0), stop=(j == ND - 1), skip_group_check=True), inc=(j == ND - 1))
        p_cw2 = V([mm, c_sw0], lambda e: e.tensor_copy(swA96[0:1, 32:64], pw1[:]))
        p_bam = V([L["birow"]], lambda e: e.reduce_max(bsc[:, 0:1], row_beta, axis=AX, apply_absolute_value=True))
        p_bg = V([p_bam], lambda e: e.tensor_single_scalar(bsc[:, 1:2], bsc[:, 0:1], 1e-30, OP.max))
        p_bre = V([p_bg], lambda e: e.reciprocal(bsc[:, 2:3], bsc[:, 1:2]))
        p_binv = V([p_bre], lambda e: e.tensor_single_scalar(bsc[:, 3:4], bsc[:, 2:3], 127.0, OP.mult))
        p_bsc = V([p_bg], lambda e: e.tensor_single_scalar(bsc[:, 4:5], bsc[:, 1:2], 1.0 / 127.0, OP.mult))
        p_bnb = V([p_bsc], lambda e: e.tensor_single_scalar(bsc[:, 5:6], bsc[:, 4:5], -128.0, OP.mult))
        p_bq1 = S([p_binv, L["birow"], c_q128], lambda e: e.activation(bqu8[:], row_beta, AF.Identity, bias=cq128[0:1, :], scale=bsc[:, 3:4]))
        p_bq2 = S([p_bq1, p_bnb], lambda e: e.activation(bqrow[:], bqu8[:], AF.Identity, bias=bsc[:, 5:6], scale=bsc[:, 4:5]))

        kqr = kq[:].rearrange("p (dj f) -> p dj f", dj=ND)
        hist = []
        for fi in range(ND):
            h = {}
            buf = kcol[fi % 2]
            prev_q0 = hist[fi - 1]["q2"] if fi >= 1 else p_bq2
            prev_k6 = hist[fi - 1]["k6"] if fi >= 1 else None
            loads = []
            for djg in range(ND):
                dep_prev = hist[fi - 2]["bwd_last"] if fi >= 2 else None
                dep_first = p_wcat[-1] if (fi <= 1 and djg == 0) else None
                loads.append(dma(f"d_k{fi % 2}", [dep_prev, dep_first],
                                 buf[:, 128 * djg:128 * (djg + 1)],
                                 dker[128 * djg:128 * (djg + 1), 128 * fi:128 * (fi + 1)]))
            egrp = []
            tlast = None
            for g in range(4):
                tt = None
                for b in range(4):
                    dj = 4 * g + b
                    deps = [loads[-1], L["eyef"]]
                    if b == 0:
                        if g == 0:
                            deps.append(hist[fi - 1]["fwd_evac"][3] if fi >= 1 else None)
                        else:
                            deps.append(egrp[g - 1])
                        if fi >= 2:
                            deps.append(hist[fi - 2]["q1"])
                    tt = T(deps, lambda e, dj=dj, b=b, buf=buf: e.transpose(
                        pb2[:, 128 * b:128 * (b + 1)], buf[:, 128 * dj:128 * (dj + 1)], eyef[:]),
                        inc=(b == 3))
                egrp.append(V([tt, prev_k6 if g == 0 else None], lambda e, g=g, fi=fi: e.tensor_copy(
                    kT[fi % 2][:, 512 * g:512 * (g + 1)], pb2[:])))
                tlast = tt
            h["fwd_evac"] = egrp
            h["fwd_last"] = tlast
            k1 = V([egrp[0], egrp[1], egrp[2], egrp[3], prev_q0, prev_k6],
                   lambda e, fi=fi: e.reduce_max(kstat[:, 4:5], kT[fi % 2][:], axis=AX, apply_absolute_value=True))
            k2 = V([k1], lambda e: e.tensor_single_scalar(kstat[:, 4:5], kstat[:, 4:5], 1e-30, OP.max))
            k3 = V([k2], lambda e: e.reciprocal(kstat[:, 5:6], kstat[:, 4:5]))
            k4 = V([k3], lambda e: e.tensor_single_scalar(kstat[:, 5:6], kstat[:, 5:6], 127.0, OP.mult))
            k5 = V([k2], lambda e: e.tensor_single_scalar(kstat[:, 6:7], kstat[:, 4:5], 1.0 / 127.0, OP.mult))
            k6 = V([k5], lambda e: e.tensor_single_scalar(kstat[:, 7:8], kstat[:, 6:7], -128.0, OP.mult))
            h["k6"] = k6
            q1 = S([k4, egrp[-1], prev_q0, c_q128],
                   lambda e, fi=fi: e.activation(ku8[:], kT[fi % 2][:], AF.Identity, bias=cq128[:], scale=kstat[:, 5:6]))
            h["q1"] = q1
            kqf = buf
            q2 = S([q1, k6, tlast], lambda e, kqf=kqf: e.activation(kqf[:], ku8[:], AF.Identity, bias=kstat[:, 7:8], scale=kstat[:, 6:7]))
            h["q2"] = q2
            bgrp = []
            blast = None
            for g in range(4):
                tt = None
                for b in range(4):
                    dj = 4 * g + b
                    deps = [q2, L["eyef"]]
                    if b == 0:
                        if g == 0:
                            deps.append(hist[fi - 1]["bwd_evac"][3] if fi >= 1 else None)
                        else:
                            deps.append(bgrp[g - 1])
                    tt = T(deps, lambda e, dj=dj, b=b, kqf=kqf: e.transpose(
                        pbwd[:, 128 * b:128 * (b + 1)], kqf[:, 128 * dj:128 * (dj + 1)], eyef[:]),
                        inc=(b == 3))
                ev = V([tt], lambda e, g=g, fi=fi: e.tensor_copy(
                    kqr[:, 4 * g:4 * g + 4, 128 * fi:128 * (fi + 1)],
                    pbwd[:].rearrange("p (a b) -> p a b", a=4)))
                bgrp.append(ev)
                blast = tt
            h["bwd_evac"] = bgrp
            h["bwd_last"] = blast
            hist.append(h)
        KQ_READY = hist[-1]["bwd_evac"][-1]
        KT0_FREE = hist[14]["q2"]
        KT1_FREE = hist[15]["q2"]
        KCOL0_FREE = hist[14]["bwd_last"]
        KCOL1_FREE = hist[15]["bwd_last"]

        R = [dict() for _ in range(NCH)]

        def emit_pro(c):
            rc = R[c]
            seq, t0 = c // CPS, (c % CPS) * CH
            xb = x_t[c % 2]
            xn = xn_t[c % 2]
            xdeps = []
            if c >= 2:
                xdeps += [R[c - 2].get("o1"), R[c - 2].get("s1"), R[c - 2].get("e2b")]
            if c == 0:
                xdeps += [p_wo, p_wT[-1]]
            if c == 1:
                xdeps += [p_wu]
            rc["xld"] = dma(f"d_x{c % 2}", xdeps, xb[:], dx[seq, t0:t0 + CH, :])
            st = st_t[c % 2]
            tb = t24[c % 2]
            nw = nw_t[c % 2]
            nwi = nwi_t[c % 2]
            dep_st = R[c - 2].get("e4b") if c >= 2 else None
            dep_st2 = R[c - 2].get("e3_1") if c >= 2 else None
            bn = []
            for q in range(4):
                bn.append(V([rc["xld"], dep_st, dep_st2], lambda e, q=q: e.bn_stats(tb[:, 6 * q:6 * q + 6], xb[:, 512 * q:512 * (q + 1)])))
            ag = V([bn[0], bn[1], bn[2], bn[3]], lambda e: e.bn_aggr(st[:, 0:2], tb[:]))
            c1_ = V([ag], lambda e: e.tensor_single_scalar(nw[:, 0:1], st[:, 1:2], EPS, OP.add))
            c2_ = V([c1_], lambda e: e.reciprocal(st[:, 5:6], nw[:, 0:1]))
            c3_ = V([c2_], lambda e: e.tensor_tensor(st[:, 6:7], st[:, 1:2], st[:, 5:6], OP.mult))
            c4_ = V([c3_], lambda e: e.tensor_single_scalar(nw[:, 1:2], st[:, 6:7], EPS, OP.add))
            rcp = V([c4_], lambda e: e.reciprocal(nw[:, 4:6], nw[:, 0:2]))
            nwd = S([rcp], lambda e: e.activation(st[:, 7:9], nw[:, 4:6], AF.Sqrt, bias=0.0, scale=1.0))
            rc["st0"] = nwd
            x1 = V([nwd], lambda e: e.tensor_tensor(st[:, 9:10], st[:, 0:1], st[:, 7:8], OP.mult))
            x2b_ = V([x1], lambda e: e.tensor_single_scalar(st[:, 9:10], st[:, 9:10], -1.0, OP.mult))
            dep_xn1 = R[c - 2].get("x2a3") if c >= 2 else None
            dep_xn2 = R[c - 2].get("q8") if c >= 2 else None
            dep_xn3 = R[c - 2].get("xnbd") if c >= 2 else None
            if c == 0:
                dep_xn1 = p_bq2
            if c == 1:
                dep_xn2 = KCOL0_FREE
            rc["xn"] = S([x2b_, rc["xld"], dep_xn1, dep_xn2, dep_xn3],
                         lambda e, xn=xn: e.activation(xn[:], xb[:], AF.Identity, bias=st[:, 9:10], scale=st[:, 7:8]))
            if TRUNC == 21:
                rc["last"] = [rc["xn"]]
                return
            dep_xnb = R[c - 1].get("t1b") if c >= 1 else None
            rc["xnbd"] = S([rc["xn"], dep_xnb], lambda e, xn=xn: e.activation(xnb_t[:], xn[:], AF.Identity, bias=0.0, scale=1.0))
            rc["am"] = V([rc["xn"]], lambda e, xn=xn: e.reduce_max(st[:, 10:11], xn[:], axis=AX, apply_absolute_value=True))
            v1 = V([rc["am"]], lambda e: e.tensor_single_scalar(st[:, 10:11], st[:, 10:11], 1e-30, OP.max))
            v2 = V([v1], lambda e: e.reciprocal(st[:, 11:12], st[:, 10:11]))
            v3 = V([v2], lambda e: e.tensor_single_scalar(st[:, 11:12], st[:, 11:12], 127.0, OP.mult))
            v4 = V([v1], lambda e: e.tensor_single_scalar(st[:, 12:13], st[:, 10:11], 1.0 / 127.0, OP.mult))
            rc["inv0"] = v3
            rc["sc0"] = v4
            dep_u8 = R[c - 1].get("qbf") if c >= 1 else p_bq2
            rc["q8"] = S([v3, dep_u8, c_q128], lambda e, xn=xn: e.activation(u8_t[:], xn[:], AF.Identity, bias=cq128[:], scale=st[:, 11:12]))
            dep_qbf = R[c - 1].get("t2b") if c >= 1 else KT0_FREE
            rc["qbf"] = S([rc["q8"], dep_qbf, c_m128], lambda e: e.activation(qbf_t[:], u8_t[:], AF.Identity, bias=cm128[:], scale=1.0))
            return

        def emit_mid(c):
            rc = R[c]
            st = st_t[c % 2]
            nw = nw_t[c % 2]
            nwi = nwi_t[c % 2]
            xn = xn_t[c % 2]
            nwd = rc["st0"]
            tb2 = [pb2, pb4]
            dep_pT0 = R[c - 1].get("t2e2") if c >= 1 else KQ_READY
            dep_pT1 = [R[c - 1].get("t2e3"), R[c - 1].get("s1")] if c >= 1 else [KQ_READY]
            dep_xnT_free = R[c - 1].get("swB") if c >= 1 else None
            t1e = []
            tlast = None
            for g in range(4):
                bank = tb2[g % 2]
                for b in range(4):
                    j = 4 * g + b
                    deps = [rc["xnbd"], L["eyeb"]]
                    if b == 0:
                        if g == 0:
                            deps.append(dep_pT0)
                        elif g == 1:
                            deps += dep_pT1
                        else:
                            deps.append(t1e[g - 2])
                    tlast = T(deps, lambda e, j=j, b=b, bank=bank: e.matmul(
                        bank[:, 128 * b:128 * (b + 1)], xnb_t[:, 128 * j:128 * (j + 1)], eyeb[:],
                        start=True, stop=True, skip_group_check=True), inc=(b == 3))
                t1e.append(V([tlast, dep_xnT_free],
                             lambda e, g=g, bank=bank: e.tensor_copy(xnT[:, 512 * g:512 * (g + 1)], bank[:])))
            rc["t1b"] = tlast
            ixb = ixqT[c % 2]
            dep_ix_free = R[c - 2].get("bigb") if c >= 2 else (KT1_FREE if c == 0 else None)
            t2e = []
            tlast = None
            for g in range(4):
                bank = tb2[g % 2]
                for b in range(4):
                    j = 4 * g + b
                    deps = [rc["qbf"], L["eyeb"]]
                    if b == 0:
                        deps.append(t1e[g + 2] if g < 2 else t2e[g - 2])
                    tlast = T(deps, lambda e, j=j, b=b, bank=bank: e.matmul(
                        bank[:, 128 * b:128 * (b + 1)], qbf_t[:, 128 * j:128 * (j + 1)], eyeb[:],
                        start=True, stop=True, skip_group_check=True), inc=(b == 3))
                t2e.append(V([tlast, dep_ix_free],
                             lambda e, g=g, ixb=ixb, bank=bank: e.tensor_copy(ixb[:, 512 * g:512 * (g + 1)], bank[:])))
            rc["t2b"] = tlast
            rc["t2e2"] = t2e[2]
            rc["t2e3"] = t2e[3]
            rc["t2e"] = t2e
            rc["ixb"] = ixb
            if TRUNC == 22:
                rc["last"] = [t2e[3], t1e[3]]
                return
            dep_prow = R[c - 1].get("rm2e") if c >= 1 else None
            dep_prow2 = R[c - 1].get("s1") if c >= 1 else None
            rc["augT"] = T([rc["inv0"], dep_prow, dep_prow2, L["eyef"]],
                           lambda e: e.transpose(prow[:], st[:, 11:12], eyef[:]), inc=True)
            dep_aug_free = R[c - 2].get("bigb") if c >= 2 else None
            rc["auge"] = S([rc["augT"], dep_aug_free],
                           lambda e: e.activation(augrow[c % 2][:], prow[:], AF.Identity, bias=0.0, scale=1.0))
            rc["r1T"] = T([nwd, rc["auge"], L["eyef"]],
                          lambda e: e.transpose(prow[:], st[:, 8:9], eyef[:]), inc=True)
            rc["r1a"] = V([rc["r1T"], c_r1a], lambda e: e.tensor_copy(r1aug[0:1, :], prow[:]))
            if TRUNC == 221:
                rc["last"] = [rc["r1a"], rc["auge"]]
                return
            dep_prep = R[c - 1].get("hf1") if c >= 1 else None
            rc["rep1"] = T([rc["r1a"], c_o32, dep_prep],
                           lambda e: e.matmul(prep1[:], ones32row[:], r1aug[0:1, :], start=True, stop=True, skip_group_check=True), inc=True)
            if TRUNC == 222:
                rc["rep1e"] = V([rc["rep1"]], lambda e: e.tensor_copy(u_sb[0:8, :], prep1[0:8, :]))
                rc["last"] = [rc["rep1e"]]
                return
            dep_u_free = R[c - 1].get("uBe") if c >= 1 else None
            swu = None
            for j in range(ND):
                deps = [t1e[j // 4], p_wcat[3 * j], p_wcat[3 * j + 1], p_wcat[3 * j + 2]]
                if j == 0:
                    deps.append(dep_u_free)
                swu = T(deps, lambda e, j=j: e.matmul(
                    pb3[0:96, 0:128], wcat[:, 96 * j:96 * (j + 1)], xnT[:, 128 * j:128 * (j + 1)],
                    start=(j == 0), stop=False, skip_group_check=True), inc=False)
            swu = T([swu, c_ones1, p_cw2], lambda e: e.matmul(
                pb3[0:96, 0:128], swA96[:], onesrow_b[:], start=False, stop=False, skip_group_check=True), inc=True)
            rc["swB"] = swu
            rc["ue"] = V([swu], lambda e: e.tensor_copy(u_sb[:], pb3[0:32, 0:128]))
            um = V([rc["ue"], rc["rep1"]], lambda e: e.tensor_tensor(u_sb[0:8, :], u_sb[0:8, :], prep1[0:8, :], OP.mult))
            uc = V([um, p_c8], lambda e: e.tensor_scalar(u_sb[0:8, :], u_sb[0:8, :], c8[:], None, OP.add))
            hb_ = h_t[c % 2]
            init = 0.0 if c % CPS == 0 else h_t[(c - 1) % 2][:, CH - 1:CH]
            dep_h = R[c - 2].get("scan") if c >= 2 else None
            rc["scan"] = V([uc, p_dr2, dep_h], lambda e, hb_=hb_: e.tensor_tensor_scan(hb_[:], decay_rep[:], u_sb[0:8, :], init, OP.mult, OP.add))
            dep_ha = R[c - 1].get("xarlast") if c >= 1 else None
            dep_ha2 = R[c - 1].get("wow") if c >= 1 else None
            rc["hb"] = V([rc["scan"], dep_ha, dep_ha2, c_haug], lambda e, hb_=hb_: e.tensor_copy(haug[0:8, :], hb_[:]))
            rc["wow"] = T([rc["hb"], p_wow, swu], lambda e: e.matmul(
                pb3[32:64, 0:128], wowdg[:], haug[0:8, :], start=False, stop=True, skip_group_check=True), inc=True)
            rc["uAe"] = V([rc["wow"]], lambda e: e.tensor_copy(uA_sb[:], pb3[32:64, 0:128]))
            rc["uBe"] = V([rc["uAe"]], lambda e: e.tensor_copy(uB_sb[:], pb3[64:96, 0:128]))
            if TRUNC == 23:
                rc["last"] = [rc["uBe"], rc["auge"]]
                return
            e1prev = R[c - 1].get("e1list") if c >= 1 else None
            prev_a = prev_b = None
            for q in range(NQ):
                deps_o = [rc["r1a"], p_orhs0, t2e[3]]
                if q == 0:
                    deps_o.append(e1prev[2] if e1prev else (KQ_READY if c == 0 else None))
                else:
                    deps_o.append(prev_a)
                mo = T(deps_o, lambda e, q=q: e.matmul(py[0][:], r1aug[:], outer_rhs[:, 512 * q:512 * (q + 1)], start=True, stop=True, skip_group_check=True), inc=True)
                deps_x = [rc["hb"], p_wo, p_wo1]
                if q == 0:
                    deps_x.append(e1prev[3] if e1prev else None)
                else:
                    deps_x.append(prev_b)
                mx = T(deps_x, lambda e, q=q: e.matmul(py[1][:], haug[:], woutaug[:, 512 * q:512 * (q + 1)], start=True, stop=True, skip_group_check=True), inc=True)
                va = V([mo, rc["xn"], prev_b], lambda e, q=q, xn=xn: e.tensor_tensor(x2tmp[:], xn[:, 512 * q:512 * (q + 1)], py[0][:], OP.mult))
                vb = V([va, mx], lambda e, q=q: e.scalar_tensor_tensor(
                    x2_t[:, 512 * q:512 * (q + 1)], x2tmp[:], 1.0, py[1][:], OP.mult, OP.add,
                    accum_out=st[:, 13 + q:14 + q]))
                prev_a, prev_b = va, vb
                rc[f"x2a{q}"] = va
                rc[f"x2b{q}"] = vb
                rc["xarlast"] = mx
            return

        def emit_mid2(c):
            rc = R[c]
            st = st_t[c % 2]
            nw = nw_t[c % 2]
            nwi = nwi_t[c % 2]
            if TRUNC == 24:
                rc["last"] = [rc["x2b3"], rc["x2a3"]]
                return
            sq = []
            for q in range(4):
                deps = [rc[f"x2b{q}"]]
                if q == 0:
                    deps.append(R[c - 1].get("e3_1") if c >= 1 else KCOL1_FREE)
                else:
                    deps.append(sq[q - 1])
                sq.append(S(deps, lambda e, q=q: e.activation(sqscr[:], x2_t[:, 512 * q:512 * (q + 1)], AF.Square,
                                                              bias=0.0, scale=1.0, accum_out=st[:, 17 + q:18 + q])))
            rc["sq1"] = sq[3]
            s1_ = V([rc["x2b3"]], lambda e: e.reduce_sum(st[:, 21:22], st[:, 13:17], axis=AX))
            s2_ = V([s1_], lambda e: e.tensor_single_scalar(st[:, 21:22], st[:, 21:22], 1.0 / D, OP.mult))
            s3_ = V([sq[3]], lambda e: e.reduce_sum(st[:, 22:23], st[:, 17:21], axis=AX))
            s4_ = V([s3_], lambda e: e.tensor_single_scalar(st[:, 22:23], st[:, 22:23], 1.0 / D, OP.mult))
            s5_ = V([s2_], lambda e: e.tensor_tensor(st[:, 23:24], st[:, 21:22], st[:, 21:22], OP.mult))
            s6_ = V([s4_, s5_], lambda e: e.tensor_tensor(st[:, 22:23], st[:, 22:23], st[:, 23:24], OP.subtract))
            s7_ = V([s6_], lambda e: e.tensor_single_scalar(nw[:, 6:7], st[:, 22:23], EPS, OP.add))
            rcp2 = V([s7_], lambda e: e.reciprocal(nw[:, 7:8], nw[:, 6:7]))
            s8_ = S([rcp2], lambda e: e.activation(st[:, 24:25], nw[:, 7:8], AF.Sqrt, bias=0.0, scale=1.0))
            s9_ = V([s8_], lambda e: e.tensor_tensor(st[:, 25:26], st[:, 24:25], st[:, 21:22], OP.mult))
            rT1 = T([s8_, rc["r1T"], L["eyef"]], lambda e: e.transpose(prow[:], st[:, 24:25], eyef[:]), inc=True)
            rc["r2e"] = V([rT1], lambda e: e.tensor_copy(r2row[:], prow[:]))
            rT2 = T([s9_, rc["r2e"], L["eyef"]], lambda e: e.transpose(prow[:], st[:, 25:26], eyef[:]), inc=True)
            rc["rm2e"] = V([rT2], lambda e: e.tensor_copy(rm2row[:], prow[:]))
            dep_pr2 = R[c - 1].get("hf1") if c >= 1 else None
            rep2 = T([rc["r2e"], c_o32, dep_pr2], lambda e: e.matmul(prep2[:], ones32row[:], r2row[:], start=True, stop=True, skip_group_check=True), inc=True)
            dep_pT2 = R[c - 1].get("hf1") if c >= 1 else None
            repT2 = T([rc["rm2e"], p_cSe, dep_pT2], lambda e: e.matmul(pT2[:], csrow[:], rm2row[:], start=True, stop=True, skip_group_check=True), inc=True)
            f1 = V([rc["uBe"], rc["rep1"], rep2], lambda e: e.tensor_tensor(hfv[:], uB_sb[:], prep1[:], OP.mult))
            f2 = V([f1, rc["uAe"]], lambda e: e.tensor_tensor(hfv[:], hfv[:], uA_sb[:], OP.add))
            f3 = V([f2, rep2], lambda e: e.tensor_tensor(hfv[:], hfv[:], prep2[:], OP.mult))
            f4 = V([f3, repT2, p_cB], lambda e: e.scalar_tensor_tensor(hfv[:], hfv[:], cB[:], pT2[:], OP.add, OP.subtract))
            dep_hfa = R[c - 1].get("wup3") if c >= 1 else None
            f5 = V([f4, dep_hfa, c_hfau], lambda e: e.tensor_scalar(hfaug[0:32, :], hfv[:], 0.0, None, OP.max))
            rc["hf1"] = f5
            if TRUNC == 25:
                rc["last"] = [f5]
                return
            prev_s = R[c - 1].get("s1") if c >= 1 else None
            wdst = [pb4, pff[0], pff[1], pff[2]]
            for q in range(NQ):
                deps = [f5, p_wu, p_wu1, prev_s]
                if q == 0:
                    deps += [rc["x2b3"], rc["t2e3"]]
                    if c == 0:
                        deps.append(KQ_READY)
                wu = T(deps, lambda e, q=q: e.matmul(
                    wdst[q][:], hfaug[:], wupaug[:, 512 * q:512 * (q + 1)], start=True, stop=True, skip_group_check=True), inc=True)
                rc[f"wup{q}"] = wu
            return

        def emit_slast(c):
            rc = R[c]
            sb_ = s_t[c % 2]
            wdst = [pb4, pff[0], pff[1], pff[2]]
            dep_s_free = R[c - 2].get("sg1") if c >= 2 else (KCOL1_FREE if c == 1 else None)
            slast = None
            for q in range(NQ):
                slast = V([rc[f"wup{q}"], dep_s_free, slast], lambda e, q=q: e.tensor_tensor(
                    sb_[:, 512 * q:512 * (q + 1)], x2_t[:, 512 * q:512 * (q + 1)], wdst[q][:], OP.add))
            rc["s1"] = slast

        def emit_big_half(c, hf_):
            rc = R[c]
            st = st_t[c % 2]
            t2e = rc["t2e"]
            ixb = rc["ixb"]
            dep_y0 = R[c - 1].get("e1b") if c >= 1 else KQ_READY
            dep_t1free = R[c - 1].get("e2b") if c >= 1 else None
            e1 = rc.setdefault("e1list", [])
            mmlast = None
            for j in range(ND):
                for qq in range(2):
                    fofs = 1024 * hf_ + 512 * qq
                    deps = [t2e[j // 4], KQ_READY]
                    if j == 0 and qq == 0:
                        deps.append(dep_y0 if hf_ == 0 else e1[-1])
                        deps += [rc["x2a3"], rc["x2b3"]]
                        if c == 0:
                            deps += [p_c8, p_cw2, p_cSe, p_wow]
                    mmlast = T(deps, lambda e, j=j, qq=qq, fofs=fofs, ixb=ixb: e.matmul(
                        py[qq][:], ixb[:, 128 * j:128 * (j + 1)],
                        kq[:, D * j + fofs:D * j + fofs + 512],
                        start=(j == 0), stop=False, skip_group_check=True), inc=False)
            for qq in range(2):
                fofs = 1024 * hf_ + 512 * qq
                mmlast = T([mmlast, rc["auge"], p_bq2], lambda e, qq=qq, fofs=fofs: e.matmul(
                    py[qq][:], augrow[c % 2][:], bqrow[:, fofs:fofs + 512],
                    start=False, stop=True, skip_group_check=True), inc=(qq == 1))
            rc[f"big{hf_}"] = mmlast

        def emit_big_evacs(c, hf_):
            rc = R[c]
            st = st_t[c % 2]
            dep_t1free = R[c - 1].get("e2b") if c >= 1 else None
            e1 = rc["e1list"]
            mmlast = rc[f"big{hf_}"]
            e1.append(V([mmlast, rc["sc0"], dep_t1free], lambda e, hf_=hf_: e.tensor_scalar(
                t1_t[:, 1024 * hf_:1024 * hf_ + 512], py[0][:], st[:, 12:13], None, OP.mult, OP.max,
                accum_out=st[:, 26 + 2 * hf_:27 + 2 * hf_])))
            e1.append(V([e1[-1]], lambda e, hf_=hf_: e.tensor_scalar(
                t1_t[:, 1024 * hf_ + 512:1024 * (hf_ + 1)], py[1][:], st[:, 12:13], None, OP.mult, OP.max,
                accum_out=st[:, 27 + 2 * hf_:28 + 2 * hf_])))
            if hf_ == 1:
                rc["bigb"] = rc["big1"]
                rc["e1b"] = e1[-1]

        def emit_echunk(c):
            rc = R[c]
            st = st_t[c % 2]
            xb = x_t[c % 2]
            sb_ = s_t[c % 2]
            seq_, t0_ = c // CPS, (c % CPS) * CH
            if TRUNC >= 21:
                for h in range(2):
                    dep_o_free = R[c - 1].get(f"store{h}") if c >= 1 else None
                    oo = V(rc["last"] + [dep_o_free], lambda e, h=h: e.tensor_copy(
                        o_t[h][:], xn_t[c % 2][:, 1024 * h:1024 * (h + 1)]))
                    rc[f"store{h}"] = dma(f"d_o{h}", [oo], dout[seq_, t0_:t0_ + CH, 1024 * h:1024 * (h + 1)], o_t[h][:])
                rc["store"] = rc["store1"]
                return
            if TRUNC == 2:
                for h in range(2):
                    dep_o_free = R[c - 1].get(f"store{h}") if c >= 1 else None
                    oo = V([rc["s1"], dep_o_free], lambda e, h=h: e.tensor_copy(
                        o_t[h][:], sb_[:, 1024 * h:1024 * (h + 1)]))
                    rc[f"sg{h}"] = oo
                    rc[f"o{h}"] = oo
                    rc[f"store{h}"] = dma(f"d_o{h}", [oo], dout[seq_, t0_:t0_ + CH, 1024 * h:1024 * (h + 1)], o_t[h][:])
                rc["store"] = rc["store1"]
                return
            if TRUNC == 3:
                e1_ = rc["e1list"]
                for h in range(2):
                    dep_o_free = R[c - 1].get(f"store{h}") if c >= 1 else None
                    oo = V([e1_[2 * h + 1], dep_o_free], lambda e, h=h: e.tensor_copy(
                        o_t[h][:], t1_t[:, 1024 * h:1024 * (h + 1)]))
                    rc[f"o{h}"] = oo
                    if h == 1:
                        rc["e2b"] = oo
                    rc[f"store{h}"] = dma(f"d_o{h}", [oo], dout[seq_, t0_:t0_ + CH, 1024 * h:1024 * (h + 1)], o_t[h][:])
                rc["store"] = rc["store1"]
                return
            e1 = rc["e1list"]
            g1 = V([e1[3]], lambda e: e.reduce_max(st[:, 30:31], st[:, 26:30], axis=AX))
            g2 = V([g1], lambda e: e.tensor_single_scalar(st[:, 30:31], st[:, 30:31], 0.0, OP.max))
            g3 = V([g2], lambda e: e.tensor_single_scalar(st[:, 30:31], st[:, 30:31], 1e-30, OP.max))
            g4 = V([g3], lambda e: e.reciprocal(st[:, 31:32], st[:, 30:31]))
            g5 = V([g4], lambda e: e.tensor_single_scalar(st[:, 31:32], st[:, 31:32], 127.0, OP.mult))
            g6 = V([g3], lambda e: e.tensor_single_scalar(st[:, 30:31], st[:, 30:31], 1.0 / 127.0, OP.mult))
            seq, t0 = c // CPS, (c % CPS) * CH
            for h in range(2):
                dep_q_free = rc.get("e3_0") if h == 1 else (R[c - 1].get("e3_1") if c >= 1 else None)
                e2 = S([g5, e1[2 * h + 1], dep_q_free], lambda e, h=h: e.activation(
                    qi_t[:], t1_t[:, 1024 * h:1024 * (h + 1)], AF.Identity, bias=0.0, scale=st[:, 31:32]))
                rc[f"e2_{h}"] = e2
                if h == 1:
                    rc["e2b"] = e2
                dep_r_free = rc.get("e4_0") if h == 1 else (R[c - 1].get("e4_1") if c >= 1 else None)
                e3 = S([e2, dep_r_free], lambda e: e.activation(r_t[:], qi_t[:], AF.Relu, bias=0.0, scale=1.0))
                rc[f"e3_{h}"] = e3
                dep_g_free = rc.get("sg0") if h == 1 else (R[c - 1].get("sg1") if c >= 1 else None)
                e4 = S([e3, g6, dep_g_free], lambda e: e.activation(
                    g_t[:], r_t[:], AF.Silu, bias=0.0, scale=st[:, 30:31]))
                rc[f"e4_{h}"] = e4
                if h == 1:
                    rc["e4b"] = e4
                sg = G([e4, rc["s1"], rc.get("tap_s")], lambda e, h=h: e.tensor_tensor(
                    sb_[:, 1024 * h:1024 * (h + 1)], g_t[:], sb_[:, 1024 * h:1024 * (h + 1)], OP.mult))
                rc[f"sg{h}"] = sg
                dep_o_free = R[c - 1].get(f"store{h}") if c >= 1 else None
                oo = G([sg, rc["xld"], dep_o_free], lambda e, h=h: e.tensor_tensor(
                    o_t[h][:], sb_[:, 1024 * h:1024 * (h + 1)],
                    xb[:, 1024 * h:1024 * (h + 1)], OP.add))
                rc[f"o{h}"] = oo
                rc[f"store{h}"] = dma(f"d_o{h}", [oo], dout[seq, t0:t0 + CH, 1024 * h:1024 * (h + 1)], o_t[h][:])
            rc["store"] = rc["store1"]

        if TRUNC == 1:
            for c in range(NCH):
                seq_, t0_ = c // CPS, (c % CPS) * CH
                for h in range(2):
                    dep = R[c - 1].get(f"store{h}") if c >= 1 else KQ_READY
                    mz = V([dep], lambda e, h=h: e.memset(o_t[h][:], 0.0))
                    R[c][f"store{h}"] = dma(f"d_o{h}", [mz], dout[seq_, t0_:t0_ + CH, 1024 * h:1024 * (h + 1)], o_t[h][:])
            R[NCH - 1]["store"] = R[NCH - 1]["store1"]
        if TRUNC == 0:
            emit_pro(0)
            for c in range(NCH):
                emit_mid(c)
                emit_big_half(c, 0)
                emit_mid2(c)
                emit_big_evacs(c, 0)
                emit_slast(c)
                emit_big_half(c, 1)
                if c + 1 < NCH:
                    emit_pro(c + 1)
                emit_big_evacs(c, 1)
                if DEBUG_TAPS and c == 0:
                    dsdbg = nc.dram_tensor("dbg_s", [128, D], F32, kind="ExternalOutput").ap()
                    R[0]["tap_s"] = dma("d_t", [R[0]["s1"]], dsdbg[:], s_t[0][:])
                emit_echunk(c)
        elif TRUNC != 1:
            for c in range(NCH):
                if c >= 1:
                    emit_echunk(c - 1)
                emit_pro(c)
                if TRUNC != 21:
                    emit_mid(c)
                    if "last" not in R[c]:
                        emit_mid2(c)
                        if "last" not in R[c]:
                            emit_slast(c)
                if TRUNC == 3:
                    emit_big_half(c, 0)
                    emit_big_evacs(c, 0)
                    emit_big_half(c, 1)
                    emit_big_evacs(c, 1)
            emit_echunk(NCH - 1)
        if DEBUG_TAPS:
            taps = [("dbg_st", st_t[0][:], F32), ("dbg_xn", xn_t[0][:], F32),
                    ("dbg_u8", u8_t[:], U8), ("dbg_ixqT", ixqT[0][:], BF16),
                    ("dbg_kq", kq[:], BF16), ("dbg_bqrow", bqrow[:], BF16),
                    ("dbg_usb", u_sb[:], F32), ("dbg_h", h_t[0][:], F32),
                    ("dbg_uA", uA_sb[:], F32), ("dbg_uB", uB_sb[:], F32),
                    ("dbg_hfv", hfv[:], F32), ("dbg_x2", x2_t[:], F32),
                    ("dbg_t1", t1_t[:], F32), ("dbg_qi", qi_t[:], I32),
                    ("dbg_g", g_t[:], F32)]
            last = R[NCH - 1]["store"]
            for nm, ap, dt_ in taps:
                dap = nc.dram_tensor(nm, list(ap.shape), dt_, kind="ExternalOutput").ap()
                last = dma("d_t", [last], dap, ap)

        P.finalize()
        with nc.Block() as block:
            @block.sync
            def _(sync):
                P.emit_engine("sync", sync, sems)
                sync.wait_ge(sems["d_o0"], 16 * NCH)
                sync.wait_ge(sems["d_o1"], 16 * NCH)

            @block.tensor
            def _(tensor):
                P.emit_engine("tensor", tensor, sems)

            @block.vector
            def _(vector):
                P.emit_engine("vector", vector, sems)

            @block.scalar
            def _(scalar):
                P.emit_engine("scalar", scalar, sems)

            @block.gpsimd
            def _(gpsimd):
                P.emit_engine("gpsimd", gpsimd, sems)

    return nc


_CACHE = {}


def _consts():
    eye_f = np.eye(128, dtype=np.float32)
    eye_b = np.eye(128, dtype=ml_dtypes.bfloat16)
    return eye_f, eye_b


def make_in_maps(inputs, n_cores=N_CORES):
    x = np.asarray(inputs["L0_attn_x"])
    B = x.shape[0]
    n_seq = B // n_cores
    eye_f, eye_b = _consts()

    def chunk_form(v):
        return np.ascontiguousarray(np.asarray(v, dtype=np.float32).reshape(ND, 128).T)

    base = {k: np.ascontiguousarray(np.asarray(v)) for k, v in inputs.items() if k != "L0_attn_x"}
    base["c_eye_f"] = eye_f
    base["c_eye_b"] = eye_b
    base["c_gamma_ch"] = chunk_form(inputs["L0_attn_ln_gamma"])
    base["c_beta_ch"] = chunk_form(inputs["L0_attn_ln_beta"])
    base["c_dskip_ch"] = chunk_form(inputs["L0_attn_D_skip"])
    base["c_bout_ch"] = chunk_form(inputs["L0_attn_b_out"])
    in_maps = []
    for i in range(n_cores):
        m = dict(base)
        m["L0_attn_x"] = np.ascontiguousarray(x[i * n_seq:(i + 1) * n_seq])
        in_maps.append(m)
    return in_maps


def kernel(**inputs):
    x = np.asarray(inputs["L0_attn_x"])
    B, S, _ = x.shape
    n_seq = B // N_CORES
    key = (n_seq, S)
    if key not in _CACHE:
        _CACHE[key] = build_nc(n_seq, S)
    nc = _CACHE[key]
    in_maps = make_in_maps(inputs)
    res = run_bass_kernel_spmd(nc, in_maps, core_ids=list(range(N_CORES)))
    return np.concatenate([r["out"] for r in res.results], axis=0)
